# version 01
# baseline (speedup 1.0000x reference)
import numpy as np
from contextlib import ExitStack

import concourse.bass as bass
import concourse.mybir as mybir
from concourse.bass_utils import run_bass_kernel_spmd

F32 = mybir.dt.float32
BF16 = mybir.dt.bfloat16
U8 = mybir.dt.uint8
AF = mybir.ActivationFunctionType
ALU = mybir.AluOpType
AX = mybir.AxisListType

NCORES = 8
D = 1024
CD = 8
T = 2048
TF = 4096
NT = T // 512
LN_EPS = 1e-5

ENGINES = ("tensor", "vector", "scalar", "gpsimd", "sync")


class P:
    def __init__(self):
        self.steps = []

    def step(self, engine, fn, deps=()):
        sid = len(self.steps)
        self.steps.append(dict(e=engine, fn=fn, deps=list(deps), dma=0))
        return sid

    def dma(self, engine, fn, n, deps=()):
        sid = len(self.steps)
        self.steps.append(dict(e=engine, fn=fn, deps=list(deps), dma=n))
        return sid

    def lower(self, nc, stack):
        seq = {e: 0 for e in ENGINES}
        dmacum = {e: 0 for e in ENGINES}
        for s in self.steps:
            e = s["e"]
            if s["dma"]:
                dmacum[e] += 16 * s["dma"]
                s["done"] = ("dma", e, dmacum[e])
            else:
                seq[e] += 1
                s["done"] = ("cmp", e, seq[e])
        sems = {e: stack.enter_context(nc.semaphore("sc_" + e)) for e in ENGINES}
        dsems = {e: stack.enter_context(nc.semaphore("sd_" + e)) for e in ENGINES}
        blk = stack.enter_context(nc.Block())
        steps = self.steps

        def emitter_for(e):
            def emit(eng):
                for s in steps:
                    if s["e"] != e:
                        continue
                    for d in s["deps"]:
                        kind, de, val = steps[d]["done"]
                        eng.wait_ge(dsems[de] if kind == "dma" else sems[de], val)
                    if s["dma"]:
                        s["fn"](eng, dsems[e])
                    else:
                        last = s["fn"](eng)
                        last.then_inc(sems[e], 1)
            return emit

        for e in ENGINES:
            getattr(blk, e)(emitter_for(e))
        return sems, dsems


def _bias_tile(b):
    return np.ascontiguousarray(np.asarray(b).reshape(-1, 128).T)


class Ctx:
    def __init__(self, nc, stack):
        self.nc = nc
        self.stack = stack

    def sb(self, name, shape, dtype):
        return self.stack.enter_context(self.nc.sbuf_tensor(name, list(shape), dtype))

    def ps(self, name, shape, dtype=F32):
        return self.stack.enter_context(self.nc.psum_tensor(name, list(shape), dtype))

    def dram_in(self, name, shape, dtype=F32):
        return self.nc.dram_tensor(name, list(shape), dtype, kind="ExternalInput").ap()

    def dram_out(self, name, shape, dtype=F32):
        return self.nc.dram_tensor(name, list(shape), dtype, kind="ExternalOutput").ap()

    def setup_consts(self, values):
        vals = list(dict.fromkeys(values))
        self._cvals = vals
        self._ctile = self.sb("constpool", [128, max(1, len(vals))], F32)

    def const_init_fn(self, v):
        last = None
        for i, val in enumerate(self._cvals):
            last = v.memset(self._ctile[:, i:i + 1], float(val))
            self.nc.const_aps.register(F32, float(val), self._ctile[:, i:i + 1])
        return last


def emit_ln(p, c, Xf, Xb, name, deps, tokens=T, nchunk=CD, psA=None, psB=None,
            ones_col=None, ones_row=None, affine=None, out=None, scratch=None,
            xb_stride=None, xb_off=0, skip_cast=False, row_buf=None):
    ntile = tokens // 512
    if xb_stride is None:
        xb_stride = tokens
    row = row_buf if row_buf is not None else c.sb(name + "_row", [1, tokens], F32)
    if scratch is None:
        scratch = c.sb(name + "_sq", [128, (4 if affine is not None else 2) * 512], BF16)
    Xsq = scratch

    if skip_cast:
        s_cast = p.step("vector", lambda v: v.engine_nop(), deps=deps)
    else:
        s_cast = p.step("vector", lambda v: v.tensor_copy(
            Xb[:].rearrange("p (c t) -> p c t", t=xb_stride)[:, 0:nchunk,
                                                             xb_off:xb_off + tokens],
            Xf[:, 0:nchunk * tokens].rearrange("p (c t) -> p c t", t=tokens)),
            deps=deps)
    mm_steps = []
    k = 0
    for ci in range(nchunk):
        for n in range(ntile):
            buf = Xsq[:, (k % 2) * 512:(k % 2) * 512 + 512]
            dd = [s_cast] if k < 2 else [s_cast, mm_steps[k - 2]]
            s_sq = p.step("vector",
                          (lambda v, ci=ci, n=n, buf=buf: v.tensor_tensor(
                              buf,
                              Xb[:, ci * xb_stride + xb_off + n * 512:
                                 ci * xb_stride + xb_off + n * 512 + 512],
                              Xb[:, ci * xb_stride + xb_off + n * 512:
                                 ci * xb_stride + xb_off + n * 512 + 512],
                              ALU.mult)),
                          deps=dd)

            def mm(t, ci=ci, n=n, buf=buf):
                t.matmul(psA[0:1, n * 512:(n + 1) * 512], ones_col[:],
                         Xb[:, ci * xb_stride + xb_off + n * 512:
                            ci * xb_stride + xb_off + (n + 1) * 512],
                         start=(ci == 0), stop=(ci == nchunk - 1))
                return t.matmul(psB[0:1, n * 512:(n + 1) * 512], ones_col[:],
                                buf, start=(ci == 0), stop=(ci == nchunk - 1))
            s_mm = p.step("tensor", mm, deps=[s_sq])
            mm_steps.append(s_mm)
            k += 1

    s_ev = p.step("vector", lambda v: v.tensor_scalar_mul(
        row[0:1, :], psB[0:1, 0:tokens], 1.0 / D), deps=[mm_steps[-1]])
    s_sqm = p.step("scalar", lambda s: s.activation(
        out=psB[0:1, 0:tokens], in_=psA[0:1, 0:tokens], func=AF.Square,
        bias=0.0, scale=1.0 / D), deps=[s_ev])
    s_var = p.step("vector", lambda v: v.tensor_tensor(
        row[0:1, :], row[0:1, :], psB[0:1, 0:tokens], ALU.subtract), deps=[s_sqm])
    s_ln = p.step("scalar", lambda s: s.activation(
        out=row[0:1, :], in_=row[0:1, :], func=AF.Ln, bias=LN_EPS, scale=1.0),
        deps=[s_var])
    s_rstd = p.step("scalar", lambda s: s.activation(
        out=row[0:1, :], in_=row[0:1, :], func=AF.Exp, bias=0.0, scale=-0.5),
        deps=[s_ln])

    def mm_bc1(t):
        last = None
        for n in range(ntile):
            last = t.matmul(psB[:, n * 512:(n + 1) * 512], ones_row[:],
                            row[0:1, n * 512:(n + 1) * 512], start=True, stop=True)
        return last
    s_bc1 = p.step("tensor", mm_bc1, deps=[s_rstd])
    s_mrs = p.step("vector", lambda v: v.scalar_tensor_tensor(
        row[0:1, :], psA[0:1, 0:tokens], 1.0 / D, row[0:1, :], ALU.mult, ALU.mult),
        deps=[s_bc1])
    def mm_bc2(t):
        last = None
        for n in range(ntile):
            last = t.matmul(psA[:, n * 512:(n + 1) * 512], ones_row[:],
                            row[0:1, n * 512:(n + 1) * 512], start=True, stop=True)
        return last
    s_bc2 = p.step("tensor", mm_bc2, deps=[s_mrs])

    if out is None:
        out = Xb

    k = 0
    sub_steps = []
    last = s_bc2
    for ci in range(nchunk):
        for n in range(ntile):
            sl = slice(ci * xb_stride + xb_off + n * 512,
                       ci * xb_stride + xb_off + n * 512 + 512)
            buf = Xsq[:, (k % 2) * 512:(k % 2) * 512 + 512]
            dd = [s_bc2] if k < 2 else [s_bc2, sub_steps[k - 2]]
            s_m = p.step("vector", (lambda v, sl=sl, buf=buf, n=n: v.tensor_tensor(
                buf, Xb[:, sl], psB[:, n * 512:(n + 1) * 512], ALU.mult)), deps=dd)
            if affine is None:
                s_s = p.step("vector", (lambda v, sl=sl, buf=buf, n=n: v.tensor_tensor(
                    out[:, sl], buf, psA[:, n * 512:(n + 1) * 512], ALU.subtract)),
                    deps=[s_m])
            else:
                buf2 = Xsq[:, (2 + k % 2) * 512:(2 + k % 2) * 512 + 512]
                s_s0 = p.step("vector", (lambda v, buf=buf, buf2=buf2, n=n:
                    v.tensor_tensor(buf2, buf, psA[:, n * 512:(n + 1) * 512],
                                    ALU.subtract)), deps=[s_m])
                s_s = p.step("vector", (lambda v, sl=sl, buf2=buf2, ci=ci:
                    v.tensor_scalar(out[:, sl], buf2, affine[0][ci], affine[1][ci],
                                    ALU.mult, ALU.add)), deps=[s_s0])
            sub_steps.append(s_s)
            last = s_s
            k += 1
    return last, row


def emit_matmul_fm(p, c, Wsb, in_sb, out_sb, cin, cout, tokens, ps_pool, deps,
                   bias=None, act=None, out_scale=None, in_tok_off=0, out_tok_off=0,
                   in_stride=None, out_stride=None, ep_extra_deps=(), ep_engine=None,
                   w_col_stride=None):
    if in_stride is None:
        in_stride = tokens
    if out_stride is None:
        out_stride = tokens
    if w_col_stride is None:
        w_col_stride = cout
    n_ci = cin // 128
    n_co = cout // 128
    ntile = tokens // 512
    groups = [(m, n) for m in range(n_co) for n in range(ntile)]
    nps = len(ps_pool)
    ep_steps = []
    for gi, (m, n) in enumerate(groups):
        psb = ps_pool[gi % nps]

        def mm(t, m=m, n=n, psb=psb):
            last = None
            for ci in range(n_ci):
                last = t.matmul(
                    psb,
                    Wsb[:, ci * w_col_stride + m * 128: ci * w_col_stride + m * 128 + 128],
                    in_sb[:, ci * in_stride + in_tok_off + n * 512:
                          ci * in_stride + in_tok_off + n * 512 + 512],
                    start=(ci == 0), stop=(ci == n_ci - 1))
            return last
        dd = list(deps)
        if gi >= nps:
            dd.append(ep_steps[gi - nps])
        s_mm = p.step("tensor", mm, deps=dd)

        osl = out_sb[:, m * out_stride + out_tok_off + n * 512:
                     m * out_stride + out_tok_off + n * 512 + 512]
        b = bias[m] if bias is not None else None

        def ep(s, psb=psb, osl=osl, b=b):
            if act is not None:
                return s.activation(out=osl, in_=psb, func=act,
                                    bias=0.0 if b is None else b,
                                    scale=1.0 if out_scale is None else out_scale)
            if b is not None:
                return s.tensor_scalar(osl, psb,
                                       1.0 if out_scale is None else out_scale, b,
                                       ALU.mult, ALU.add)
            if out_scale is not None:
                return s.tensor_scalar_mul(osl, psb, out_scale)
            return s.tensor_copy(osl, psb)
        eng = ep_engine or ("scalar" if act is not None else "vector")
        s_ep = p.step(eng, ep, deps=[s_mm] + list(ep_extra_deps))
        ep_steps.append(s_ep)
    return ep_steps


def build_f0():
    nc = bass.Bass("TRN2", target_bir_lowering=False)
    stack = ExitStack()
    c = Ctx(nc, stack)
    p = P()

    xin = c.dram_in("xin", [D, T])
    Wr = c.dram_in("Wr", [D, 8])
    br = c.dram_in("br", [8])
    W1 = c.dram_in("W1", [8, D, 2048])
    b1f = c.dram_in("b1f", [8, 128, 16])
    W2 = c.dram_in("W2", [8, 2048, D])
    b2f = c.dram_in("b2f", [8, 128, 8])
    ident_in = c.dram_in("ident", [128, 128])
    xout = c.dram_out("xout", [D, T])

    c.setup_consts([LN_EPS, 0.0])
    Xf = c.sb("Xf", [128, CD * T], F32)
    Xb = c.sb("Xb", [128, CD * T], BF16)
    psA = c.ps("psA", [128, T])
    psB = c.ps("psB", [128, T])
    ones_col = c.sb("ones_col", [128, 1], BF16)
    ones_row = c.sb("ones_row", [1, 128], F32)
    ones_rb = c.sb("ones_rb", [1, 128], BF16)
    identb = c.sb("identb", [128, 128], F32)
    Wrs = c.sb("Wrs", [128, 8 * 8], BF16)
    brs = c.sb("brs", [1, 8], F32)
    LTs = c.sb("LTs", [128, 16 * 8], F32)
    route = c.sb("route", [128, 16 * 8], F32)
    route2 = c.sb("route2", [128, 16 * 8], F32)
    rsc = c.sb("rsc", [128, 16 * 4], F32)
    grows = c.sb("grows", [8, T], BF16)
    grow_cur = c.sb("grow_cur", [1, T], BF16)
    Gcur = c.sb("Gcur", [128, T], BF16)
    W1s = c.sb("W1s", [128, CD * 2048], BF16)
    W2s = c.sb("W2s", [128, 16 * D], BF16)
    b1s = c.sb("b1s", [128, 16], F32)
    b2s = c.sb("b2s", [128, 8], F32)
    He = c.sb("He", [128, 16 * 512], BF16)
    Ye = c.sb("Ye", [128, 8 * 512], BF16)

    s_init = p.step("vector", lambda v: (v.memset(ones_col[:], 1.0),
                                         v.memset(ones_row[:], 1.0),
                                         v.memset(ones_rb[:], 1.0))[-1])
    s_cst = p.step("vector", lambda v: c.const_init_fn(v))


    s_ldx = p.dma("sync", lambda e, ds: e.dma_start(
        out=Xf[:].rearrange("p (c t) -> p c t", t=T),
        in_=xin.rearrange("(c p) t -> p c t", p=128)).then_inc(ds, 16), 1)
    s_ldr = p.dma("gpsimd", lambda e, ds: (
        e.dma_start(out=Wrs[:].rearrange("p (c n) -> p c n", n=8),
                    in_=Wr.rearrange("(c p) n -> p c n", p=128)).then_inc(ds, 16),
        e.dma_start(out=brs[:], in_=br[None, :]).then_inc(ds, 16),
        e.dma_start(out=identb[:], in_=ident_in).then_inc(ds, 16))[-1], 3)

    s_h, lnrow = emit_ln(p, c, Xf, Xb, "ln", deps=[s_ldx, s_init, s_cst], psA=psA,
                         psB=psB, ones_col=ones_col, ones_row=ones_row, scratch=He)

    def mm_logits(t):
        last = None
        for n in range(16):
            for ci in range(CD):
                t.matmul(psB[:, n * 8:(n + 1) * 8],
                         Xb[:, ci * T + n * 128: ci * T + n * 128 + 128],
                         Wrs[:, ci * 8:(ci + 1) * 8],
                         start=(ci == 0), stop=False)
            last = t.matmul(psB[:, n * 8:(n + 1) * 8], ones_row[:], brs[:],
                            start=False, stop=True)
        return last
    s_lg = p.step("tensor", mm_logits, deps=[s_h, s_ldr])
    s_lgc = p.step("vector", lambda v: v.tensor_copy(LTs[:], psB[:, 0:128]),
                   deps=[s_lg])

    def route_v1(v):
        last = None
        for n in range(16):
            last = v.tensor_reduce(rsc[:, n * 4: n * 4 + 1],
                                   LTs[:, n * 8:(n + 1) * 8], AX.X, ALU.max,
                                   negate=True)
        return last
    s_r1 = p.step("vector", route_v1, deps=[s_lgc])

    def route_s2(s):
        last = None
        for n in range(16):
            last = s.activation(out=route[:, n * 8:(n + 1) * 8],
                                in_=LTs[:, n * 8:(n + 1) * 8], func=AF.Exp,
                                bias=rsc[:, n * 4 + 0: n * 4 + 1], scale=1.0,
                                accum_out=rsc[:, n * 4 + 1: n * 4 + 2])
        return last
    s_r3 = p.step("scalar", route_s2, deps=[s_r1])

    def route_va(v):
        last = None
        for n in range(16):
            last = v.reciprocal(rsc[:, n * 4 + 2: n * 4 + 3],
                                rsc[:, n * 4 + 1: n * 4 + 2])
        return last
    s_ra = p.step("vector", route_va, deps=[s_r3])

    def route_vb(v):
        last = None
        for n in range(16):
            last = v.tensor_scalar(route[:, n * 8:(n + 1) * 8],
                                   LTs[:, n * 8:(n + 1) * 8],
                                   rsc[:, n * 4: n * 4 + 1], 0.0,
                                   ALU.add, ALU.is_ge)
        return last
    s_rb = p.step("vector", route_vb, deps=[s_r3])

    def route_vc(v):
        last = None
        for n in range(16):
            last = v.tensor_scalar(route2[:, n * 8:(n + 1) * 8],
                                   route[:, n * 8:(n + 1) * 8],
                                   rsc[:, n * 4 + 2: n * 4 + 3], 0.5,
                                   ALU.mult, ALU.mult)
        return last
    s_r4 = p.step("vector", route_vc, deps=[s_ra, s_rb])

    def tr_route(t):
        last = None
        for n in range(16):
            last = t.transpose(psA[0:8, n * 128:(n + 1) * 128],
                               route2[:, n * 8:(n + 1) * 8], identb[:])
        return last
    s_tr = p.step("tensor", tr_route, deps=[s_r4, s_ldr])
    s_gr = p.step("vector", lambda v: v.tensor_copy(grows[:], psA[0:8, 0:T]),
                  deps=[s_tr])

    ps_banks = [psA[:, k * 512:(k + 1) * 512] for k in range(4)] + \
               [psB[:, k * 512:(k + 1) * 512] for k in range(4)]

    prev_expert_tail = [s_gr, s_ldx]
    for e in range(8):
        def ld_we(g, ds, e=e):
            g.dma_start(out=W1s[:].rearrange("p (c f) -> p c f", f=2048),
                        in_=W1[e].rearrange("(c p) f -> p c f", p=128)).then_inc(ds, 16)
            g.dma_start(out=W2s[:].rearrange("p (c f) -> p c f", f=D),
                        in_=W2[e].rearrange("(c p) f -> p c f", p=128)).then_inc(ds, 16)
            g.dma_start(out=b1s[:], in_=b1f[e]).then_inc(ds, 16)
            return g.dma_start(out=b2s[:], in_=b2f[e]).then_inc(ds, 16)
        s_ldw = p.dma("gpsimd", ld_we, 4, deps=prev_expert_tail)

        s_gcp = p.dma("sync", (lambda e2, ds, e=e: e2.dma_start(
            out=grow_cur[:], in_=grows[e:e + 1, :]).then_inc(ds, 16)), 1,
            deps=[s_gr] + ([prev_ge] if e > 0 else []))

        def mm_ge(t, e=e):
            last = None
            for n in range(NT):
                last = t.matmul(psB[:, n * 512:(n + 1) * 512], ones_rb[:],
                                grow_cur[0:1, n * 512:(n + 1) * 512],
                                start=True, stop=True)
            return last
        s_ge = p.step("tensor", mm_ge, deps=prev_expert_tail + [s_gcp])
        prev_ge = s_ge
        s_gec = p.step("vector", lambda v: v.tensor_copy(Gcur[:], psB[:, 0:T]),
                       deps=[s_ge])

        tails = []
        prev_cm = None
        for n in range(NT):
            h_eps = emit_matmul_fm(
                p, c, W1s, Xb, He, D, 2048, 512, ps_banks[0:2],
                deps=[s_ldw, s_h, s_gec], bias=[b1s[:, m:m + 1] for m in range(16)],
                act=AF.Gelu, in_tok_off=n * 512, in_stride=T, out_stride=512)
            y_eps = emit_matmul_fm(
                p, c, W2s, He, Ye, 2048, D, 512, ps_banks[2:4],
                deps=h_eps, bias=[b2s[:, m:m + 1] for m in range(8)],
                in_stride=512, out_stride=512,
                ep_extra_deps=([prev_cm] if prev_cm is not None else []))

            def comb_mult(g, n=n):
                last = None
                for ci in range(CD):
                    last = g.tensor_tensor(Ye[:, ci * 512:(ci + 1) * 512],
                                           Ye[:, ci * 512:(ci + 1) * 512],
                                           Gcur[:, n * 512:(n + 1) * 512], ALU.mult)
                return last
            s_cm = p.step("gpsimd", comb_mult, deps=y_eps + [s_gec])

            def comb_add(v, n=n):
                last = None
                for ci in range(CD):
                    sl = slice(ci * T + n * 512, ci * T + n * 512 + 512)
                    last = v.tensor_tensor(Xf[:, sl], Xf[:, sl],
                                           Ye[:, ci * 512:(ci + 1) * 512], ALU.add)
                return last
            s_ca = p.step("vector", comb_add, deps=[s_cm])
            prev_cm = s_cm
            tails.append(s_ca)
        tails.append(prev_cm)
        prev_expert_tail = tails

    s_st = p.dma("sync", lambda e, ds: e.dma_start(
        out=xout.rearrange("(c p) t -> p c t", p=128),
        in_=Xf[:].rearrange("p (c t) -> p c t", t=T)).then_inc(ds, 16), 1,
        deps=prev_expert_tail)
    p.step("sync", lambda e: e.nop(), deps=[s_st])

    p.lower(nc, stack)
    return nc, stack


def build_f1():
    nc = bass.Bass("TRN2", target_bir_lowering=False)
    stack = ExitStack()
    c = Ctx(nc, stack)
    p = P()
    c.setup_consts([LN_EPS, 0.0])

    xin = c.dram_in("xin", [D, T])
    k0 = c.dram_in("k0", [2048, 4096], U8)
    k1 = c.dram_in("k1", [4096, 4096], U8)
    k2 = c.dram_in("k2", [4096, D], U8)
    s012 = c.dram_in("s012", [1, 3])
    xout = c.dram_out("xout", [D, T])

    Xf = c.sb("Xf", [128, CD * T], F32)
    Xb = c.sb("Xb", [128, CD * T], BF16)
    psA = c.ps("psA", [128, T])
    psB = c.ps("psB", [128, T])
    ones_col = c.sb("ones_col", [128, 1], BF16)
    ones_row = c.sb("ones_row", [1, 128], F32)
    scrow = c.sb("scrow", [1, 3], F32)
    scv = c.sb("scv", [128, 1], F32)
    W0a = c.sb("W0a", [128, CD * 128], U8)
    W0b = c.sb("W0b", [128, CD * 128], U8)
    W0d = c.sb("W0d", [128, 2 * CD * 128], BF16)
    W1u = c.sb("W1u", [128, 32 * 128], U8)
    W1d = c.sb("W1d", [128, 32 * 128], BF16)
    W2u = c.sb("W2u", [128, 32 * 128], U8)
    W2d = c.sb("W2d", [128, 32 * 128], BF16)
    R1 = c.sb("R1", [128, 32 * 512], BF16)
    R2 = c.sb("R2", [128, 32 * 512], BF16)

    s_init = p.step("vector", lambda v: (v.memset(ones_col[:], 1.0),
                                         v.memset(ones_row[:], 1.0))[-1])
    s_cst = p.step("vector", lambda v: c.const_init_fn(v))
    s_ldx = p.dma("sync", lambda e, ds: e.dma_start(
        out=Xf[:].rearrange("p (c t) -> p c t", t=T),
        in_=xin.rearrange("(c p) t -> p c t", p=128)).then_inc(ds, 16), 1)
    s_lds = p.dma("gpsimd", lambda e, ds: e.dma_start(
        out=scrow[:], in_=s012).then_inc(ds, 16), 1)

    s_sc1 = p.step("vector", lambda v: v.tensor_tensor(
        scrow[0:1, 0:1], scrow[0:1, 0:1], scrow[0:1, 1:2], ALU.mult), deps=[s_lds])
    s_sc2 = p.step("vector", lambda v: v.scalar_tensor_tensor(
        scrow[0:1, 2:3], scrow[0:1, 2:3], 4.0, scrow[0:1, 0:1], ALU.mult, ALU.mult),
        deps=[s_sc1])
    s_sc3 = p.step("tensor", lambda t: t.matmul(
        psA[:, 0:1], ones_row[:], scrow[0:1, 2:3], start=True, stop=True),
        deps=[s_sc2, s_ldx])
    s_sc4 = p.step("vector", lambda v: v.tensor_copy(scv[:], psA[:, 0:1]), deps=[s_sc3])

    s_h, _lnrow = emit_ln(p, c, Xf, Xb, "ln", deps=[s_ldx, s_init, s_cst, s_sc4],
                  psA=psA, psB=psB, ones_col=ones_col, ones_row=ones_row, scratch=R1)

    ps_banks = [psA[:, k * 512:(k + 1) * 512] for k in range(4)] + \
               [psB[:, k * 512:(k + 1) * 512] for k in range(4)]

    prev_tile_end = [s_h]
    for n in range(NT):
        l1_eps = []
        prev_ld = None
        for m in range(32):
            pb = m % 2
            def ld0(g, ds, m=m, pb=pb):
                g.dma_start(out=W0a[:].rearrange("p (c f) -> p c f", f=128),
                            in_=k0[0:1024, m * 128:(m + 1) * 128]
                            .rearrange("(c p) f -> p c f", p=128)).then_inc(ds, 16)
                return g.dma_start(out=W0b[:].rearrange("p (c f) -> p c f", f=128),
                                   in_=k0[1024:2048, m * 128:(m + 1) * 128]
                                   .rearrange("(c p) f -> p c f", p=128)).then_inc(ds, 16)
            dd = list(prev_tile_end) if prev_ld is None else [prev_ld]
            s_l0 = p.dma("gpsimd", ld0, 2, deps=dd)
            s_cv = p.step("vector", (lambda v, pb=pb: v.tensor_tensor(
                W0d[:, pb * 1024:(pb + 1) * 1024], W0a[:], W0b[:], ALU.subtract)),
                deps=[s_l0] + ([l1_eps[m - 2]] if m >= 2 else []))
            prev_ld = s_cv

            def mm(t, m=m, n=n, pb=pb):
                last = None
                for ci in range(CD):
                    last = t.matmul(ps_banks[m % 4],
                                    W0d[:, pb * 1024 + ci * 128: pb * 1024 + ci * 128 + 128],
                                    Xb[:, ci * T + n * 512: ci * T + n * 512 + 512],
                                    start=(ci == 0), stop=(ci == CD - 1))
                return last
            s_mm = p.step("tensor", mm,
                          deps=[s_cv, s_h] + ([l1_eps[m - 4]] if m >= 4 else list(prev_tile_end)))
            s_ep = p.step("scalar", (lambda s, m=m: s.activation(
                out=R1[:, m * 512:(m + 1) * 512], in_=ps_banks[m % 4], func=AF.Relu,
                bias=0.0, scale=1.0)), deps=[s_mm])
            l1_eps.append(s_ep)

        l2_eps = []
        prev_ld = None
        prev_mm = None
        for m in range(32):
            pb = m % 2
            def ld1(g, ds, m=m):
                return g.dma_start(out=W1u[:].rearrange("p (c f) -> p c f", f=128),
                                   in_=k1[:, m * 128:(m + 1) * 128]
                                   .rearrange("(c p) f -> p c f", p=128)).then_inc(ds, 16)
            dd = l1_eps if prev_ld is None else [prev_ld]
            s_l1 = p.dma("gpsimd", ld1, 1, deps=dd)
            s_cv = p.step("vector", (lambda v: v.tensor_scalar(
                W1d[:], W1u[:], 2.0, -1.0, ALU.mult, ALU.add)),
                deps=[s_l1] + ([prev_mm] if m >= 1 else []))
            prev_ld = s_cv

            def mm(t, m=m):
                last = None
                for ci in range(32):
                    last = t.matmul(ps_banks[4 + m % 2],
                                    W1d[:, ci * 128: ci * 128 + 128],
                                    R1[:, ci * 512:(ci + 1) * 512],
                                    start=(ci == 0), stop=(ci == 31))
                return last
            s_mm = p.step("tensor", mm,
                          deps=[s_cv] + l1_eps[-1:] + ([l2_eps[m - 2]] if m >= 2 else []))
            prev_mm = s_mm
            s_ep = p.step("scalar", (lambda s, m=m: s.activation(
                out=R2[:, m * 512:(m + 1) * 512], in_=ps_banks[4 + m % 2], func=AF.Relu,
                bias=0.0, scale=1.0)), deps=[s_mm])
            l2_eps.append(s_ep)

        l3_eps = []
        prev_ld = None
        prev_mm3 = None
        for m in range(CD):
            pb = m % 2
            def ld2(g, ds, m=m):
                return g.dma_start(out=W2u[:].rearrange("p (c f) -> p c f", f=128),
                                   in_=k2[:, m * 128:(m + 1) * 128]
                                   .rearrange("(c p) f -> p c f", p=128)).then_inc(ds, 16)
            dd = l2_eps if prev_ld is None else [prev_ld]
            s_l2 = p.dma("gpsimd", ld2, 1, deps=dd)
            s_cv = p.step("vector", (lambda v: v.tensor_scalar(
                W2d[:], W2u[:], 2.0, -1.0, ALU.mult, ALU.add)),
                deps=[s_l2] + ([prev_mm3] if m >= 1 else []))
            prev_ld = s_cv

            def mm(t, m=m):
                last = None
                for ci in range(32):
                    last = t.matmul(ps_banks[6 + m % 2],
                                    W2d[:, ci * 128: ci * 128 + 128],
                                    R2[:, ci * 512:(ci + 1) * 512],
                                    start=(ci == 0), stop=(ci == 31))
                return last
            s_mm = p.step("tensor", mm,
                          deps=[s_cv] + l2_eps[-1:] + ([l3_eps[m - 2]] if m >= 2 else []))
            prev_mm3 = s_mm
            s_ep = p.step("vector", (lambda v, m=m, n=n: v.scalar_tensor_tensor(
                Xf[:, m * T + n * 512: m * T + n * 512 + 512],
                ps_banks[6 + m % 2], scv[:, 0:1],
                Xf[:, m * T + n * 512: m * T + n * 512 + 512],
                ALU.mult, ALU.add)), deps=[s_mm, s_sc4])
            l3_eps.append(s_ep)
        prev_tile_end = l3_eps

    s_st = p.dma("sync", lambda e, ds: e.dma_start(
        out=xout.rearrange("(c p) t -> p c t", p=128),
        in_=Xf[:].rearrange("p (c t) -> p c t", t=T)).then_inc(ds, 16), 1,
        deps=prev_tile_end)
    p.step("sync", lambda e: e.nop(), deps=[s_st])
    p.lower(nc, stack)
    return nc, stack


def f1_inputs(x_fm_cores, ins):
    k0 = np.ascontiguousarray(np.asarray(ins["L1_ffn_k0"]).view(np.uint8))
    k1 = np.ascontiguousarray(np.asarray(ins["L1_ffn_k1"]).view(np.uint8))
    k2 = np.ascontiguousarray(np.asarray(ins["L1_ffn_k2"]).view(np.uint8))
    s012 = np.ascontiguousarray(np.stack([np.asarray(ins["L1_ffn_s0"]),
                                          np.asarray(ins["L1_ffn_s1"]),
                                          np.asarray(ins["L1_ffn_s2"])]).reshape(1, 3)
                                .astype(np.float32))
    return [dict(xin=x_fm_cores[i], k0=k0, k1=k1, k2=k2, s012=s012)
            for i in range(NCORES)]


def build_f2():
    nc = bass.Bass("TRN2", target_bir_lowering=False)
    stack = ExitStack()
    c = Ctx(nc, stack)
    p = P()
    c.setup_consts([LN_EPS, 0.0])

    xin = c.dram_in("xin", [D, T])
    Wg = c.dram_in("Wg", [D, 16])
    W1 = c.dram_in("W1", [16, D, 2048])
    b1f = c.dram_in("b1f", [16, 128, 16])
    W2 = c.dram_in("W2", [16, 2048, D])
    b2f = c.dram_in("b2f", [16, 128, 8])
    ident_in = c.dram_in("ident", [128, 128])
    xout = c.dram_out("xout", [D, T])

    Xf = c.sb("Xf", [128, CD * T], F32)
    Xb = c.sb("Xb", [128, CD * T], BF16)
    psA = c.ps("psA", [128, T])
    psB = c.ps("psB", [128, T])
    ones_col = c.sb("ones_col", [128, 1], BF16)
    ones_row = c.sb("ones_row", [1, 128], F32)
    ones_rb = c.sb("ones_rb", [1, 128], BF16)
    identb = c.sb("identb", [128, 128], F32)
    Wgs = c.sb("Wgs", [128, 8 * 16], BF16)
    LTs = c.sb("LTs", [128, 16 * 16], F32)
    route = c.sb("route", [128, 16 * 16], F32)
    rsc = c.sb("rsc", [128, 16 * 12], F32)
    grows = c.sb("grows", [16, T], BF16)
    Gcur = c.sb("Gcur", [128, T], BF16)
    W1s = c.sb("W1s", [128, CD * 2048], BF16)
    W2s = c.sb("W2s", [128, 16 * D], BF16)
    b1s = c.sb("b1s", [128, 16], F32)
    b2s = c.sb("b2s", [128, 8], F32)
    He = c.sb("He", [128, 16 * 512], BF16)
    Ye = c.sb("Ye", [128, 8 * 512], BF16)

    s_init = p.step("vector", lambda v: (v.memset(ones_col[:], 1.0),
                                         v.memset(ones_row[:], 1.0),
                                         v.memset(ones_rb[:], 1.0))[-1])
    s_cst = p.step("vector", lambda v: c.const_init_fn(v))

    s_ldx = p.dma("sync", lambda e, ds: e.dma_start(
        out=Xf[:].rearrange("p (c t) -> p c t", t=T),
        in_=xin.rearrange("(c p) t -> p c t", p=128)).then_inc(ds, 16), 1)
    s_ldr = p.dma("gpsimd", lambda e, ds: (
        e.dma_start(out=Wgs[:].rearrange("p (c n) -> p c n", n=16),
                    in_=Wg.rearrange("(c p) n -> p c n", p=128)).then_inc(ds, 16),
        e.dma_start(out=identb[:], in_=ident_in).then_inc(ds, 16))[-1], 2)

    s_h, lnrow = emit_ln(p, c, Xf, Xb, "ln", deps=[s_ldx, s_init, s_cst], psA=psA,
                         psB=psB, ones_col=ones_col, ones_row=ones_row, scratch=He)

    def mm_logits(t):
        last = None
        for n in range(16):
            for ci in range(CD):
                last = t.matmul(psB[:, n * 16:(n + 1) * 16],
                                Xb[:, ci * T + n * 128: ci * T + n * 128 + 128],
                                Wgs[:, ci * 16:(ci + 1) * 16],
                                start=(ci == 0), stop=(ci == CD - 1))
        return last
    s_lg = p.step("tensor", mm_logits, deps=[s_h, s_ldr])
    s_lgc = p.step("vector", lambda v: v.tensor_copy(LTs[:], psB[:, 0:256]),
                   deps=[s_lg])

    def route_v1(v):
        last = None
        for n in range(16):
            last = v.tensor_reduce(rsc[:, n * 12: n * 12 + 1],
                                   LTs[:, n * 16:(n + 1) * 16], AX.X, ALU.max,
                                   negate=True)
        return last
    s_r1 = p.step("vector", route_v1, deps=[s_lgc])

    def route_s2(s):
        last = None
        for n in range(16):
            last = s.activation(out=route[:, n * 16:(n + 1) * 16],
                                in_=LTs[:, n * 16:(n + 1) * 16], func=AF.Exp,
                                bias=rsc[:, n * 12: n * 12 + 1], scale=1.0)
        return last
    s_r3 = p.step("scalar", route_s2, deps=[s_r1])

    def route_m8(v):
        last = None
        for n in range(16):
            last = v.max(rsc[:, n * 12 + 4: n * 12 + 12],
                         route[:, n * 16:(n + 1) * 16])
        return last
    s_m8 = p.step("vector", route_m8, deps=[s_r3])

    def route_sum3(v):
        last = None
        for n in range(16):
            last = v.tensor_reduce(rsc[:, n * 12 + 1: n * 12 + 2],
                                   rsc[:, n * 12 + 4: n * 12 + 7], AX.X, ALU.add)
        return last
    s_s3 = p.step("vector", route_sum3, deps=[s_m8])

    def route_rec(v):
        last = None
        for n in range(16):
            last = v.reciprocal(rsc[:, n * 12 + 2: n * 12 + 3],
                                rsc[:, n * 12 + 1: n * 12 + 2])
        return last
    s_rc = p.step("vector", route_rec, deps=[s_s3])

    def route_mask(v):
        last = None
        for n in range(16):
            last = v.tensor_scalar(LTs[:, n * 16:(n + 1) * 16],
                                   route[:, n * 16:(n + 1) * 16],
                                   rsc[:, n * 12 + 6: n * 12 + 7], None, ALU.is_ge)
        return last
    s_mk = p.step("vector", route_mask, deps=[s_m8])

    def route_me(v):
        last = None
        for n in range(16):
            last = v.tensor_tensor(LTs[:, n * 16:(n + 1) * 16],
                                   LTs[:, n * 16:(n + 1) * 16],
                                   route[:, n * 16:(n + 1) * 16], ALU.mult)
        return last
    s_me = p.step("vector", route_me, deps=[s_mk])

    def route_g(v):
        last = None
        for n in range(16):
            last = v.tensor_scalar(LTs[:, n * 16:(n + 1) * 16],
                                   LTs[:, n * 16:(n + 1) * 16],
                                   rsc[:, n * 12 + 2: n * 12 + 3], 0.5,
                                   ALU.mult, ALU.mult)
        return last
    s_r4 = p.step("vector", route_g, deps=[s_me, s_rc])

    def tr_route(t):
        last = None
        for n in range(16):
            last = t.transpose(psA[0:16, n * 128:(n + 1) * 128],
                               LTs[:, n * 16:(n + 1) * 16], identb[:])
        return last
    s_tr = p.step("tensor", tr_route, deps=[s_r4, s_ldr])
    s_gr = p.step("vector", lambda v: v.tensor_copy(grows[:], psA[0:16, 0:T]),
                  deps=[s_tr])

    ps_banks = [psA[:, k * 512:(k + 1) * 512] for k in range(4)] + \
               [psB[:, k * 512:(k + 1) * 512] for k in range(4)]

    prev_expert_tail = [s_gr, s_ldx]
    prev_ge = None
    for e in range(16):
        def ld_we(g, ds, e=e):
            g.dma_start(out=W1s[:].rearrange("p (c f) -> p c f", f=2048),
                        in_=W1[e].rearrange("(c p) f -> p c f", p=128)).then_inc(ds, 16)
            g.dma_start(out=W2s[:].rearrange("p (c f) -> p c f", f=D),
                        in_=W2[e].rearrange("(c p) f -> p c f", p=128)).then_inc(ds, 16)
            g.dma_start(out=b1s[:], in_=b1f[e]).then_inc(ds, 16)
            return g.dma_start(out=b2s[:], in_=b2f[e]).then_inc(ds, 16)
        s_ldw = p.dma("gpsimd", ld_we, 4, deps=prev_expert_tail)

        s_gcp = p.dma("gpsimd", (lambda e2, ds, e=e: e2.dma_start(
            out=lnrow[0:1, :], in_=grows[e:e + 1, :]).then_inc(ds, 16)), 1,
            deps=[s_gr] + ([prev_ge] if prev_ge is not None else []))

        def mm_ge(t, e=e):
            last = None
            for n in range(NT):
                last = t.matmul(psB[:, n * 512:(n + 1) * 512], ones_row[:],
                                lnrow[0:1, n * 512:(n + 1) * 512],
                                start=True, stop=True)
            return last
        s_ge = p.step("tensor", mm_ge, deps=prev_expert_tail + [s_gcp])
        prev_ge = s_ge
        s_gec = p.step("vector", lambda v: v.tensor_copy(Gcur[:], psB[:, 0:T]),
                       deps=[s_ge])

        tails = []
        prev_cm = None
        for n in range(NT):
            h_eps = emit_matmul_fm(
                p, c, W1s, Xb, He, D, 2048, 512, ps_banks[0:2],
                deps=[s_ldw, s_h, s_gec], bias=[b1s[:, m:m + 1] for m in range(16)],
                act=AF.Relu, in_tok_off=n * 512, in_stride=T, out_stride=512)
            y_eps = emit_matmul_fm(
                p, c, W2s, He, Ye, 2048, D, 512, ps_banks[2:4],
                deps=h_eps, bias=[b2s[:, m:m + 1] for m in range(8)],
                in_stride=512, out_stride=512,
                ep_extra_deps=([prev_cm] if prev_cm is not None else []))

            def comb_mult(g, n=n):
                last = None
                for ci in range(CD):
                    last = g.tensor_tensor(Ye[:, ci * 512:(ci + 1) * 512],
                                           Ye[:, ci * 512:(ci + 1) * 512],
                                           Gcur[:, n * 512:(n + 1) * 512], ALU.mult)
                return last
            s_cm = p.step("gpsimd", comb_mult, deps=y_eps + [s_gec])

            def comb_add(v, n=n):
                last = None
                for ci in range(CD):
                    sl = slice(ci * T + n * 512, ci * T + n * 512 + 512)
                    last = v.tensor_tensor(Xf[:, sl], Xf[:, sl],
                                           Ye[:, ci * 512:(ci + 1) * 512], ALU.add)
                return last
            s_ca = p.step("vector", comb_add, deps=[s_cm])
            prev_cm = s_cm
            tails.append(s_ca)
        tails.append(prev_cm)
        prev_expert_tail = tails

    s_st = p.dma("sync", lambda e, ds: e.dma_start(
        out=xout.rearrange("(c p) t -> p c t", p=128),
        in_=Xf[:].rearrange("p (c t) -> p c t", t=T)).then_inc(ds, 16), 1,
        deps=prev_expert_tail)
    p.step("sync", lambda e: e.nop(), deps=[s_st])
    p.lower(nc, stack)
    return nc, stack


def f2_inputs(x_fm_cores, ins):
    Wg = np.ascontiguousarray(ins["L2_ffn_gate"], np.float32)
    W1 = np.ascontiguousarray(ins["L2_ffn_W1"], np.float32)
    b1f = np.ascontiguousarray(np.asarray(ins["L2_ffn_b1"]).reshape(16, 16, 128)
                               .transpose(0, 2, 1), np.float32)
    W2 = np.ascontiguousarray(ins["L2_ffn_W2"], np.float32)
    b2f = np.ascontiguousarray(np.asarray(ins["L2_ffn_b2"]).reshape(16, 8, 128)
                               .transpose(0, 2, 1), np.float32)
    ident = np.eye(128, dtype=np.float32)
    return [dict(xin=x_fm_cores[i], Wg=Wg, W1=W1, b1f=b1f, W2=W2, b2f=b2f,
                 ident=ident) for i in range(NCORES)]


_FP4_MID = [0.25, 0.75, 1.25, 1.75, 2.5, 3.5, 5.0]
_FP4_STEP = [0.5, 0.5, 0.5, 0.5, 1.0, 1.0, 2.0]


def emit_fp4qd(p, c, X, XQ, Y, Q, SR, ncols, deps, name=""):
    nb = ncols // 16
    Xv = X.rearrange("p (b k) -> p b k", k=16)
    s_a = p.step("vector", lambda v: v.tensor_reduce(
        SR[:, 0:nb], Xv, AX.X, ALU.max, apply_absolute_value=True), deps=deps)
    s_s = p.step("vector", lambda v: v.tensor_scalar(
        SR[:, 0:nb], SR[:, 0:nb], 1.0 / 6.0, 1e-12, ALU.mult, ALU.max), deps=[s_a])
    s_r = p.step("vector", lambda v: v.reciprocal(SR[:, nb:2 * nb], SR[:, 0:nb]),
                 deps=[s_s])
    s_xn = p.step("vector", lambda v: v.tensor_tensor(
        XQ.rearrange("p (b k) -> p b k", k=16), Xv,
        SR[:, nb:2 * nb, None].broadcast_to([128, nb, 16]), ALU.mult), deps=[s_r])
    s_y = p.step("scalar", lambda s: s.activation(out=Y, in_=XQ, func=AF.Abs,
                                                  bias=0.0, scale=1.0), deps=[s_xn])
    s_q0 = p.step("vector", lambda v: v.tensor_single_scalar(
        Q, Y, _FP4_MID[0], ALU.is_gt), deps=[s_y])
    s_q = p.step("scalar", lambda s: s.mul(Q, Q, _FP4_STEP[0]), deps=[s_q0])
    for i in range(1, 7):
        s_t = p.step("vector", (lambda e, i=i: e.tensor_single_scalar(
            XQ, Y, _FP4_MID[i], ALU.is_gt)), deps=[s_y, s_q])
        if _FP4_STEP[i] != 1.0:
            s_t = p.step("scalar", (lambda s, i=i: s.mul(XQ, XQ, _FP4_STEP[i])),
                         deps=[s_t])
        s_q = p.step("gpsimd", (lambda e: e.tensor_tensor(Q, Q, XQ, ALU.add)),
                     deps=[s_t, s_q])
    s_sg = p.step("scalar", lambda s: s.activation(out=XQ, in_=X, func=AF.Sign,
                                                   bias=0.0, scale=1.0), deps=[s_q])
    s_m1 = p.step("vector", lambda v: v.tensor_tensor(XQ, XQ, Q, ALU.mult),
                  deps=[s_sg])
    s_m2 = p.step("gpsimd", lambda g: g.tensor_tensor(
        XQ.rearrange("p (b k) -> p b k", k=16),
        XQ.rearrange("p (b k) -> p b k", k=16),
        SR[:, 0:nb, None].broadcast_to([128, nb, 16]), ALU.mult), deps=[s_m1, s_s])
    return s_m2


F3_TG = 256
F3_I = 2816


def build_f3():
    nc = bass.Bass("TRN2", target_bir_lowering=False)
    stack = ExitStack()
    c = Ctx(nc, stack)
    p = P()
    c.setup_consts([LN_EPS, 0.0])

    xin = c.dram_in("xin", [D, T])
    wg_in = c.dram_in("wg", [F3_I, D])
    wu_in = c.dram_in("wu", [F3_I, D])
    wd_in = c.dram_in("wd", [D, F3_I])
    ident_in = c.dram_in("ident", [128, 128])
    xout = c.dram_out("xout", [D, T])
    wgt = nc.dram_tensor("wgt_scr", [D, F3_I], BF16, kind="Internal").ap()
    wut = nc.dram_tensor("wut_scr", [D, F3_I], BF16, kind="Internal").ap()
    wdt = nc.dram_tensor("wdt_scr", [F3_I, D], BF16, kind="Internal").ap()

    Xf = c.sb("Xf", [128, CD * T], F32)
    Xb = c.sb("Xb", [128, CD * T], BF16)
    Xq = c.sb("Xq", [128, CD * T], BF16)
    psA = c.ps("psA", [128, T])
    psB = c.ps("psB", [128, T])
    ones_col = c.sb("ones_col", [128, 1], BF16)
    ones_row = c.sb("ones_row", [1, 128], F32)
    identb = c.sb("identb", [128, 128], BF16)
    GB = c.sb("GB", [128, 2 * F3_I], BF16)
    UB = c.sb("UB", [128, 2 * F3_I], BF16)
    HQ = c.sb("HQ", [128, 2 * F3_I], BF16)
    YB = c.sb("YB", [128, 2 * F3_I], BF16)
    QB = c.sb("QB", [128, 2 * F3_I], BF16)
    SR = c.sb("SR", [128, 2 * (2 * F3_I) // 16], F32)
    WS = c.sb("WS", [128, 4096], BF16)

    s_init = p.step("vector", lambda v: (v.memset(ones_col[:], 1.0),
                                         v.memset(ones_row[:], 1.0))[-1])
    s_cst = p.step("vector", lambda v: c.const_init_fn(v))
    s_ldi = p.dma("gpsimd", lambda e, ds: e.dma_start(
        out=identb[:], in_=ident_in).then_inc(ds, 16), 1)

    s_ldx = p.dma("sync", lambda e, ds: e.dma_start(
        out=Xf[:].rearrange("p (c t) -> p c t", t=T),
        in_=xin.rearrange("(c p) t -> p c t", p=128)).then_inc(ds, 16), 1)

    s_h, lnrow = emit_ln(p, c, Xf, Xb, "ln", deps=[s_ldx, s_init, s_cst], psA=psA,
                         psB=psB, ones_col=ones_col, ones_row=ones_row, scratch=GB)

    wsteps = []
    for wname, w_in, w_scr, nin, nout in (
            ("wg", wg_in, wgt, D, F3_I), ("wu", wu_in, wut, D, F3_I),
            ("wd", wd_in, wdt, F3_I, D)):
        nrt = nout // 128
        nic = nin // 128
        prev = None
        for rt in range(nrt):
            sl_g = GB[:, 0:nin]
            s_ld = p.dma("gpsimd", (lambda e, ds, w_in=w_in, rt=rt, sl_g=sl_g, nin=nin:
                         e.dma_start(out=sl_g, in_=w_in[rt * 128:(rt + 1) * 128, :])
                         .then_inc(ds, 16)), 1,
                         deps=([prev] if prev is not None else [s_h]))
            s_qd = emit_fp4qd(p, c, GB[:, 0:nin], UB[:, 0:nin], YB[:, 0:nin],
                              QB[:, 0:nin], SR, nin, deps=[s_ld], name=wname + str(rt))
            evs = []
            prev_ev = None
            for c0 in range(0, nic, 4):
                cw = min(4, nic - c0)
                def trg(t, c0=c0, cw=cw):
                    last = None
                    for j in range(cw):
                        last = t.transpose(
                            psA.bitcast(BF16)[:, j * 1024: j * 1024 + 128],
                            UB[:, (c0 + j) * 128:(c0 + j + 1) * 128], identb[:])
                    return last
                dd = [s_qd, s_ldi] + ([prev_ev] if prev_ev is not None else [])
                s_t1 = p.step("tensor", trg, deps=dd)
                def evg(v, c0=c0, cw=cw):
                    last = None
                    for j in range(cw):
                        last = v.tensor_copy(
                            HQ[:, (c0 + j) * 128:(c0 + j + 1) * 128],
                            psA.bitcast(BF16)[:, j * 1024: j * 1024 + 128])
                    return last
                s_e1 = p.step("vector", evg, deps=[s_t1])
                prev_ev = s_e1
                evs.append(s_e1)
            s_st = p.dma("gpsimd", (lambda e, ds, w_scr=w_scr, rt=rt, nic=nic:
                         e.dma_start(out=w_scr[:, rt * 128:(rt + 1) * 128]
                                     .rearrange("(c p) f -> p c f", p=128),
                                     in_=HQ[:].rearrange("p (c f) -> p c f", f=128)[:, 0:nic, :])
                         .then_inc(ds, 16)), 1, deps=evs[-2:])
            prev = s_st
            wsteps.append(s_st)

    xq_steps = []
    prevq = None
    for g in range(16):
        def trh(t, g=g):
            last = None
            for ci in range(CD):
                last = t.transpose(psB.bitcast(BF16)[:, ci * 256: ci * 256 + 128],
                                   Xb[:, ci * T + g * 128: ci * T + g * 128 + 128],
                                   identb[:])
            return last
        dd = [s_h, s_ldi] + ([prevq] if prevq is not None else wsteps[-1:])
        s_trh = p.step("tensor", trh, deps=dd)
        s_evh = p.step("vector", lambda v: v.tensor_copy(
            GB[:, 0:D].rearrange("p (c f) -> p c f", f=128),
            psB.bitcast(BF16)[:].rearrange("p (c f) -> p c f", f=256)[:, 0:CD, 0:128]),
            deps=[s_trh])
        s_qd = emit_fp4qd(p, c, GB[:, 0:D], UB[:, 0:D], YB[:, 0:D], QB[:, 0:D],
                          SR, D, deps=[s_evh], name="xq%d" % g)
        def trb(t, g=g):
            last = None
            for ci in range(CD):
                last = t.transpose(psB.bitcast(BF16)[:, ci * 256: ci * 256 + 128],
                                   UB[:, ci * 128:(ci + 1) * 128], identb[:])
            return last
        s_trb = p.step("tensor", trb, deps=[s_qd])
        s_evb = p.step("vector", (lambda v, g=g: v.tensor_copy(
            Xq[:].rearrange("p (c t) -> p c t", t=T)[:, 0:CD, g * 128:(g + 1) * 128],
            psB.bitcast(BF16)[:].rearrange("p (c f) -> p c f", f=256)[:, 0:CD, 0:128])),
            deps=[s_trb])
        prevq = s_evb
        xq_steps.append(s_evb)

    ps_banks = [psA[:, k * 512:(k + 1) * 512] for k in range(4)] + \
               [psB[:, k * 512:(k + 1) * 512] for k in range(4)]

    prev_tail = xq_steps + wsteps
    for g in range(8):
        tok0 = g * 256
        gu_eps = []
        prev_ld = None
        for wi, (w_scr, OB) in enumerate(((wgt, GB), (wut, UB))):
            for o in range(0, F3_I, 512):
                ow = min(512, F3_I - o)
                def ldw(e, ds, w_scr=w_scr, o=o, ow=ow):
                    return e.dma_start(out=WS[:, 0:CD * ow]
                                       .rearrange("p (c f) -> p c f", f=ow),
                                       in_=w_scr[:, o:o + ow]
                                       .rearrange("(c p) f -> p c f", p=128))\
                        .then_inc(ds, 16)
                dd = [prev_ld] if prev_ld is not None else list(prev_tail)
                s_lw = p.dma("sync", ldw, 1, deps=dd)
                mmeps = []
                for s2 in range(2):
                    def mm(t, o=o, ow=ow, s2=s2, tok0=tok0):
                        last = None
                        for ci in range(CD):
                            last = t.matmul(
                                ps_banks[s2][:, 0:ow],
                                Xq[:, ci * T + tok0 + s2 * 128: ci * T + tok0 + s2 * 128 + 128],
                                WS[:, ci * ow: ci * ow + ow],
                                start=(ci == 0), stop=(ci == CD - 1))
                        return last
                    s_mm = p.step("tensor", mm, deps=[s_lw] + gu_eps[-2:])
                    s_ep = p.step("scalar", (lambda s, OB=OB, o=o, ow=ow, s2=s2:
                                  s.activation(out=OB[:, s2 * F3_I + o: s2 * F3_I + o + ow],
                                               in_=ps_banks[s2][:, 0:ow], func=AF.Identity,
                                               bias=0.0, scale=1.0)), deps=[s_mm])
                    mmeps.append(s_ep)
                gu_eps.extend(mmeps)
                prev_ld = mmeps[-1]
        s_si = p.step("scalar", lambda s: s.activation(
            out=YB[:], in_=GB[:], func=AF.Silu, bias=0.0, scale=1.0), deps=gu_eps)
        s_hm = p.step("vector", lambda v: v.tensor_tensor(GB[:], YB[:], UB[:], ALU.mult),
                      deps=[s_si])
        s_qd = emit_fp4qd(p, c, GB[:], HQ[:], YB[:], QB[:], SR, 2 * F3_I,
                          deps=[s_hm], name="hq%d" % g)
        ev_steps = []
        prev_ev2 = None
        pairs = [(s2, ci) for s2 in range(2) for ci in range(22)]
        for k0 in range(0, len(pairs), 4):
            grp = pairs[k0:k0 + 4]
            def trg(t, grp=grp):
                last = None
                for j, (s2, ci) in enumerate(grp):
                    last = t.transpose(
                        psA.bitcast(BF16)[:, j * 1024: j * 1024 + 128],
                        HQ[:, s2 * F3_I + ci * 128: s2 * F3_I + ci * 128 + 128],
                        identb[:])
                return last
            dd = [s_qd] + ([prev_ev2] if prev_ev2 is not None else [])
            s_t1 = p.step("tensor", trg, deps=dd)
            def evg(v, grp=grp):
                last = None
                for j, (s2, ci) in enumerate(grp):
                    last = v.tensor_copy(
                        YB[:, ci * 256 + s2 * 128: ci * 256 + s2 * 128 + 128],
                        psA.bitcast(BF16)[:, j * 1024: j * 1024 + 128])
                return last
            s_e1 = p.step("vector", evg, deps=[s_t1])
            prev_ev2 = s_e1
            ev_steps.append(s_e1)
        dn_eps = []
        prev_lw = None
        for m in range(CD):
            def ldw2(e, ds, m=m):
                return e.dma_start(out=WS[:, 0:22 * 128]
                                   .rearrange("p (c f) -> p c f", f=128),
                                   in_=wdt[:, m * 128:(m + 1) * 128]
                                   .rearrange("(c p) f -> p c f", p=128)).then_inc(ds, 16)
            dd = [prev_lw] if prev_lw is not None else [gu_eps[-1]]
            s_lw = p.dma("sync", ldw2, 1, deps=dd + ev_steps[-1:])
            def mm2(t, m=m):
                last = None
                for ci in range(22):
                    last = t.matmul(ps_banks[4 + m % 2][:, 0:256],
                                    WS[:, ci * 128:(ci + 1) * 128],
                                    YB[:, ci * 256: ci * 256 + 256],
                                    start=(ci == 0), stop=(ci == 21))
                return last
            s_mm = p.step("tensor", mm2, deps=[s_lw] + ([dn_eps[m - 2]] if m >= 2 else []))
            def ep2(v, m=m, tok0=tok0):
                return v.scalar_tensor_tensor(
                    Xf[:, m * T + tok0: m * T + tok0 + 256],
                    ps_banks[4 + m % 2][:, 0:256], 0.5,
                    Xf[:, m * T + tok0: m * T + tok0 + 256], ALU.mult, ALU.add)
            s_ep = p.step("vector", ep2, deps=[s_mm])
            dn_eps.append(s_ep)
            prev_lw = s_lw
        prev_tail = dn_eps

    s_st = p.dma("sync", lambda e, ds: e.dma_start(
        out=xout.rearrange("(c p) t -> p c t", p=128),
        in_=Xf[:].rearrange("p (c t) -> p c t", t=T)).then_inc(ds, 16), 1,
        deps=prev_tail)
    p.step("sync", lambda e: e.nop(), deps=[s_st])
    p.lower(nc, stack)
    return nc, stack


def f3_inputs(x_fm_cores, ins):
    wg = np.ascontiguousarray(ins["L3_ffn_w_gate"], np.float32)
    wu = np.ascontiguousarray(ins["L3_ffn_w_up"], np.float32)
    wd = np.ascontiguousarray(ins["L3_ffn_w_down"], np.float32)
    ident = np.eye(128, dtype=np.float32)
    return [dict(xin=x_fm_cores[i], wg=wg, wu=wu, wd=wd, ident=ident)
            for i in range(NCORES)]


def build_a2(phase=99):
    nc = bass.Bass("TRN2", target_bir_lowering=False)
    stack = ExitStack()
    c = Ctx(nc, stack)
    p = P()
    c.setup_consts([LN_EPS, 0.0])

    xloc = c.dram_in("xloc", [D, T])
    xoth = c.dram_in("xoth", [D, T])
    WQ = c.dram_in("WQ", [16, D, 64])
    WK = c.dram_in("WK", [16, D, 64])
    WV = c.dram_in("WV", [16, D, 64])
    bQ = c.dram_in("bQ", [16, 64])
    bK = c.dram_in("bK", [16, 64])
    bV = c.dram_in("bV", [16, 64])
    Wp = c.dram_in("Wp", [D, D])
    bp_t = c.dram_in("bp_t", [128, 8])
    ident_in = c.dram_in("ident", [128, 128])
    xout = c.dram_out("xout", [D, T])

    loff = 0
    ooff = T

    Xf = c.sb("Xf", [128, CD * T], F32)
    Xb = c.sb("Xb", [128, CD * TF], BF16)
    psF = c.ps("psF", [128, TF])
    psA = psF[:, 0:T]
    psB = psF[:, T:2 * T]
    ones_col = c.sb("ones_col", [128, 1], BF16)
    ones_row = c.sb("ones_row", [1, 128], F32)
    identb = c.sb("identb", [128, 128], BF16)
    Wqs = c.sb("Wqs", [128, CD * 64], BF16)
    Wks = c.sb("Wks", [128, CD * 64], BF16)
    Wvs = c.sb("Wvs", [128, CD * 64], BF16)
    bqs = c.sb("bqs", [64, 1], F32)
    bks = c.sb("bks", [64, 1], F32)
    bvs = c.sb("bvs", [1, 64], F32)
    qs = c.sb("qs", [64, T], BF16)
    ks = c.sb("ks", [64, TF], BF16)
    vtm = c.sb("vtm", [128, 32 * 64], BF16)
    probs = c.sb("probs", [128, TF], BF16)
    PT = c.sb("PT", [128, 16 * 128], BF16)
    avtmp = c.sb("avtmp", [64, T], BF16)
    AVb = c.sb("AVb", [128, CD * T], BF16)
    srow = c.sb("srow", [128, 2], F32)
    Wps = c.sb("Wps", [128, CD * 128], BF16)
    bps = c.sb("bps", [128, 8], F32)

    s_init = p.step("vector", lambda v: (v.memset(ones_col[:], 1.0),
                                         v.memset(ones_row[:], 1.0))[-1])
    s_cst = p.step("vector", lambda v: c.const_init_fn(v))

    s_ldx = p.dma("sync", lambda e, ds: e.dma_start(
        out=Xf[:].rearrange("p (c t) -> p c t", t=T),
        in_=xloc.rearrange("(c p) t -> p c t", p=128)).then_inc(ds, 16), 1)
    s_ldo = p.dma("gpsimd", lambda e, ds: e.dma_start(
        out=Xb[:].rearrange("p (c t) -> p c t", t=TF)[:, :, ooff:ooff + T],
        in_=xoth.rearrange("(c p) t -> p c t", p=128)).then_inc(ds, 16), 1)
    s_ldw = p.dma("gpsimd", lambda e, ds: (
        e.dma_start(out=identb[:], in_=ident_in).then_inc(ds, 16),
        e.dma_start(out=bps[:], in_=bp_t).then_inc(ds, 16))[-1], 2)

    s_h1, lnrow = emit_ln(p, c, Xf, Xb, "lnl", deps=[s_ldx, s_init, s_cst],
                          psA=psA, psB=psB, ones_col=ones_col, ones_row=ones_row,
                          scratch=AVb, xb_stride=TF, xb_off=loff)
    s_h2, _ = emit_ln(p, c, Xf, Xb, "lno", deps=[s_ldo, s_init, s_cst, s_h1],
                      psA=psA, psB=psB, ones_col=ones_col, ones_row=ones_row,
                      scratch=AVb, xb_stride=TF, xb_off=ooff, skip_cast=True,
                      row_buf=lnrow)
    s_h = s_h2

    out_eps_all = []
    prev_head_tail = [s_h]
    if phase == 0:
        s_dmp = p.step("vector", lambda v: v.tensor_copy(Xf[:, 0:2 * T], Xb[:, 0:TF]),
                       deps=[s_h])
        stack2.pop_all() if False else None
        s_st = p.dma("sync", lambda e, ds: e.dma_start(
            out=xout.rearrange("(c p) t -> p c t", p=128),
            in_=Xf[:].rearrange("p (c t) -> p c t", t=T)).then_inc(ds, 16), 1,
            deps=[s_dmp])
        p.step("sync", lambda e: e.nop(), deps=[s_st])
        p.lower(nc, stack)
        return nc, stack
    for h in range(16):
        def ld_wh(g, ds, h=h):
            g.dma_start(out=Wqs[:].rearrange("p (c f) -> p c f", f=64),
                        in_=WQ[h].rearrange("(c p) f -> p c f", p=128)).then_inc(ds, 16)
            g.dma_start(out=Wks[:].rearrange("p (c f) -> p c f", f=64),
                        in_=WK[h].rearrange("(c p) f -> p c f", p=128)).then_inc(ds, 16)
            g.dma_start(out=Wvs[:].rearrange("p (c f) -> p c f", f=64),
                        in_=WV[h].rearrange("(c p) f -> p c f", p=128)).then_inc(ds, 16)
            g.dma_start(out=bqs[:], in_=bQ[h][:, None]).then_inc(ds, 16)
            g.dma_start(out=bks[:], in_=bK[h][:, None]).then_inc(ds, 16)
            return g.dma_start(out=bvs[:], in_=bV[h][None, :]).then_inc(ds, 16)
        s_ldh = p.dma("gpsimd", ld_wh, 6, deps=prev_head_tail)
        s_bq8 = p.step("vector", lambda v: v.tensor_scalar_mul(bqs[:], bqs[:], 8.0),
                       deps=[s_ldh])

        q_eps = []
        for n in range(NT):
            bk = (n % 2) * 512
            def mmq(t, n=n, bk=bk):
                last = None
                for ci in range(CD):
                    last = t.matmul(psF[0:64, bk:bk + 512],
                                    Wqs[:, ci * 64:(ci + 1) * 64],
                                    Xb[:, ci * TF + loff + n * 512: ci * TF + loff + n * 512 + 512],
                                    start=(ci == 0), stop=(ci == CD - 1))
                return last
            s_mm = p.step("tensor", mmq,
                          deps=[s_ldh, s_h] + ([q_eps[n - 2]] if n >= 2 else prev_head_tail))
            s_ep = p.step("scalar", (lambda s, n=n, bk=bk: s.activation(
                out=qs[:, n * 512:(n + 1) * 512], in_=psF[0:64, bk:bk + 512],
                func=AF.Identity, bias=bqs[:], scale=8.0)), deps=[s_mm, s_bq8])
            q_eps.append(s_ep)
        k_eps = []
        for n in range(TF // 512):
            bk = 1024 + (n % 2) * 512
            def mmk(t, n=n, bk=bk):
                last = None
                for ci in range(CD):
                    last = t.matmul(psF[0:64, bk:bk + 512],
                                    Wks[:, ci * 64:(ci + 1) * 64],
                                    Xb[:, ci * TF + n * 512: ci * TF + n * 512 + 512],
                                    start=(ci == 0), stop=(ci == CD - 1))
                return last
            s_mm = p.step("tensor", mmk,
                          deps=[s_ldh, s_h] + ([k_eps[n - 2]] if n >= 2 else prev_head_tail))
            s_ep = p.step("scalar", (lambda s, n=n, bk=bk: s.activation(
                out=ks[:, n * 512:(n + 1) * 512], in_=psF[0:64, bk:bk + 512],
                func=AF.Identity, bias=bks[:], scale=1.0)), deps=[s_mm])
            k_eps.append(s_ep)
        v_eps = []
        for g2 in range(32):
            bk = (4 + g2 % 4) * 512
            def mmv(t, g2=g2, bk=bk):
                last = None
                for ci in range(CD):
                    last = t.matmul(psF[:, bk:bk + 64],
                                    Xb[:, ci * TF + g2 * 128: ci * TF + g2 * 128 + 128],
                                    Wvs[:, ci * 64:(ci + 1) * 64],
                                    start=(ci == 0), stop=False)
                last = t.matmul(psF[:, bk:bk + 64],
                                ones_row[:], bvs[:], start=False, stop=True)
                return last
            s_mm = p.step("tensor", mmv,
                          deps=[s_ldh, s_h] + ([v_eps[g2 - 4]] if g2 >= 4 else prev_head_tail))
            s_ep = p.step("vector", (lambda v, g2=g2, bk=bk: v.tensor_copy(
                vtm[:, g2 * 64:(g2 + 1) * 64], psF[:, bk:bk + 64])),
                deps=[s_mm])
            v_eps.append(s_ep)

        head_tail = []
        prev_sc = None
        if phase == 1:
            head_tail = [v_eps[-1], q_eps[-1], k_eps[-1]]
        for qt in range(16 if phase > 1 else 0):
            def mmsc(t, qt=qt):
                last = None
                for kc in range(8):
                    last = t.matmul(psF[:, kc * 512:(kc + 1) * 512],
                                    qs[:, qt * 128:(qt + 1) * 128],
                                    ks[:, kc * 512:(kc + 1) * 512],
                                    start=True, stop=True)
                return last
            dd = q_eps[-2:] + k_eps[-2:] + v_eps[-4:]
            if prev_sc is not None:
                dd = dd + [prev_sc]
            s_sc = p.step("tensor", mmsc, deps=dd)
            s_mx = p.step("vector", (lambda v: v.tensor_reduce(
                srow[:, 0:1], psF[:, 0:TF], AX.X, ALU.max, negate=True)), deps=[s_sc])
            s_ex = p.step("scalar", (lambda s: s.activation(
                out=probs[:], in_=psF[:, 0:TF], func=AF.Exp, bias=srow[:, 0:1],
                scale=1.0, accum_out=srow[:, 1:2])), deps=[s_mx])
            s_rc = p.step("vector", lambda v: v.reciprocal(srow[:, 1:2], srow[:, 1:2]),
                          deps=[s_ex])
            s_nm = p.step("vector", (lambda v: v.tensor_scalar(
                probs[:], probs[:], srow[:, 1:2], None, ALU.mult)), deps=[s_rc])
            if phase == 2:
                s_ae = p.step("vector", (lambda v, qt=qt: v.tensor_copy(
                    avtmp[:, qt * 128:(qt + 1) * 128], probs[0:64, 0:128])), deps=[s_nm])
                prev_sc = s_ae
                head_tail.append(s_ae)
                continue
            s_av = None
            s_ae0 = None
            prev_ev = None
            for wv in range(2):
                ev_steps = []
                for gr in range(2):
                    def trg(t, wv=wv, gr=gr):
                        last = None
                        for j in range(8):
                            last = t.transpose(
                                psF.bitcast(BF16)[:, j * 1024 + 256: j * 1024 + 384],
                                probs[:, (wv * 16 + gr * 8 + j) * 128:
                                      (wv * 16 + gr * 8 + j + 1) * 128],
                                identb[:])
                        return last
                    dd2 = [s_nm] + ([prev_ev] if prev_ev is not None else []) + \
                          ([s_av] if s_av is not None else []) + \
                          ([s_ae0] if (wv == 1 and s_av is not None) else [])
                    s_tr = p.step("tensor", trg, deps=dd2)
                    def evg(v, gr=gr):
                        last = None
                        for j in range(8):
                            last = v.tensor_copy(
                                PT[:, (gr * 8 + j) * 128: (gr * 8 + j + 1) * 128],
                                psF.bitcast(BF16)[:, j * 1024 + 256: j * 1024 + 384])
                        return last
                    s_ev = p.step("vector", evg, deps=[s_tr])
                    prev_ev = s_ev
                    ev_steps.append(s_ev)
                def mmav(t, wv=wv):
                    last = None
                    for g2 in range(16):
                        last = t.matmul(psF[0:64, 0:128],
                                        vtm[:, (wv * 16 + g2) * 64:(wv * 16 + g2 + 1) * 64],
                                        PT[:, g2 * 128:(g2 + 1) * 128],
                                        start=(g2 == 0), stop=(g2 == 15))
                    return last
                if phase == 3:
                    s_av = None
                    continue
                s_av = p.step("tensor", mmav, deps=ev_steps + ([s_ae0] if wv else []))
                if wv == 0:
                    s_ae0 = p.step("vector", (lambda v, qt=qt: v.tensor_copy(
                        avtmp[:, qt * 128:(qt + 1) * 128], psF[0:64, 0:128])),
                        deps=[s_av])
            if phase == 3:
                s_ae = p.step("vector", (lambda v, qt=qt: v.tensor_copy(
                    avtmp[:, qt * 128:(qt + 1) * 128], PT[0:64, 0:128])), deps=ev_steps)
            else:
                s_ae = p.step("vector", (lambda v, qt=qt: v.tensor_tensor(
                    avtmp[:, qt * 128:(qt + 1) * 128], avtmp[:, qt * 128:(qt + 1) * 128],
                    psF[0:64, 0:128], ALU.add)), deps=[s_av, s_ae0])
            prev_sc = s_ae
            head_tail.append(s_ae)
        if phase == 1:
            pass
        s_mv = p.dma("sync", (lambda e, ds, h=h: e.dma_start(
            out=AVb[(h % 2) * 64:(h % 2) * 64 + 64, (h // 2) * T:(h // 2 + 1) * T],
            in_=avtmp[:]).then_inc(ds, 16)), 1, deps=head_tail[-1:])
        prev_head_tail = [s_mv]
        out_eps_all.append(s_mv)
        if phase == 4:
            break

    if phase == 5:
        s_cpd = p.step("vector", lambda v: v.tensor_copy(Xf[:], AVb[:]),
                       deps=out_eps_all)
        s_st = p.dma("sync", lambda e, ds: e.dma_start(
            out=xout.rearrange("(c p) t -> p c t", p=128),
            in_=Xf[:].rearrange("p (c t) -> p c t", t=T)).then_inc(ds, 16), 1,
            deps=[s_cpd])
        p.step("sync", lambda e: e.nop(), deps=[s_st])
        p.lower(nc, stack)
        return nc, stack
    if phase == 4:
        def dumps(v):
            v.tensor_copy(Xf[:, 0:T], AVb[0:128, 0:T])
            v.tensor_copy(Xf[0:64, T:2 * T], qs[:, 0:T])
            v.tensor_copy(Xf[0:64, 2 * T:4 * T], ks[:, 0:TF])
            v.tensor_copy(Xf[:, 4 * T:6 * T], Xb[:, 0:TF])
            v.tensor_copy(Xf[:, 6 * T:7 * T], probs[:, 0:T])
            return v.tensor_copy(Xf[0:64, 7 * T:7 * T + 2048], avtmp[:, 0:2048])
        s_cpd = p.step("vector", dumps, deps=out_eps_all)
        s_st = p.dma("sync", lambda e, ds: e.dma_start(
            out=xout.rearrange("(c p) t -> p c t", p=128),
            in_=Xf[:].rearrange("p (c t) -> p c t", t=T)).then_inc(ds, 16), 1,
            deps=[s_cpd])
        p.step("sync", lambda e: e.nop(), deps=[s_st])
        p.lower(nc, stack)
        return nc, stack

    prev_lw = None
    oeps = []
    for m in range(CD):
        def ldwp(e, ds, m=m):
            return e.dma_start(out=Wps[:].rearrange("p (c f) -> p c f", f=128),
                               in_=Wp[:, m * 128:(m + 1) * 128]
                               .rearrange("(c p) f -> p c f", p=128)).then_inc(ds, 16)
        s_lw = p.dma("gpsimd", ldwp, 1,
                     deps=([prev_mm_o] if prev_lw is not None else out_eps_all))
        prev_lw = s_lw
        for n in range(NT):
            def mmo(t, m=m, n=n):
                last = None
                for ci in range(CD):
                    last = t.matmul(psF[:, ((m * NT + n) % 4) * 512: ((m * NT + n) % 4) * 512 + 512],
                                    Wps[:, ci * 128:(ci + 1) * 128],
                                    AVb[:, ci * T + n * 512: ci * T + n * 512 + 512],
                                    start=(ci == 0), stop=(ci == CD - 1))
                return last
            gi = m * NT + n
            s_mm = p.step("tensor", mmo,
                          deps=[s_lw] + out_eps_all[-1:] + ([oeps[gi - 4]] if gi >= 4 else []))
            prev_mm_o = s_mm
            def epo(v, m=m, n=n):
                sl = slice(m * T + n * 512, m * T + n * 512 + 512)
                return v.scalar_tensor_tensor(
                    Xf[:, sl],
                    psF[:, ((m * NT + n) % 4) * 512: ((m * NT + n) % 4) * 512 + 512],
                    bps[:, m:m + 1], Xf[:, sl], ALU.add, ALU.add)
            s_ep = p.step("vector", epo, deps=[s_mm, s_ldw])
            oeps.append(s_ep)

    s_st = p.dma("sync", lambda e, ds: e.dma_start(
        out=xout.rearrange("(c p) t -> p c t", p=128),
        in_=Xf[:].rearrange("p (c t) -> p c t", t=T)).then_inc(ds, 16), 1, deps=oeps)
    p.step("sync", lambda e: e.nop(), deps=[s_st])
    p.lower(nc, stack)
    return nc, stack


def a2_inputs(x_fm_cores, ins):
    Wqkv = np.asarray(ins["L2_attn_Wqkv"], np.float32)
    WQ = np.ascontiguousarray(Wqkv[0:16])
    WK = np.ascontiguousarray(Wqkv[16:32])
    WV = np.ascontiguousarray(Wqkv[32:48])
    bqkv = np.asarray(ins["L2_attn_bqkv"], np.float32)
    bQ = np.ascontiguousarray(bqkv[0:16])
    bK = np.ascontiguousarray(bqkv[16:32])
    bV = np.ascontiguousarray(bqkv[32:48])
    Wp = np.ascontiguousarray(np.asarray(ins["L2_attn_Wp"], np.float32).T)
    bp_t = _bias_tile(np.asarray(ins["L2_attn_bp"], np.float32))
    ident = np.eye(128, dtype=np.float32)
    out = []
    for cc in range(NCORES):
        other = cc ^ 1
        out.append(dict(xloc=x_fm_cores[cc], xoth=x_fm_cores[other],
                        WQ=WQ, WK=WK, WV=WV, bQ=bQ, bK=bK, bV=bV, Wp=Wp,
                        bp_t=bp_t, ident=ident))
    return out


def build_a0(phase=99):
    nc = bass.Bass("TRN2", target_bir_lowering=False)
    stack = ExitStack()
    c = Ctx(nc, stack)
    p = P()
    c.setup_consts([LN_EPS, 0.0])

    xloc = c.dram_in("xloc", [D, T])
    xoth = c.dram_in("xoth", [D, T])
    Wdq = c.dram_in("Wdq", [D, 512])
    Wdkv = c.dram_in("Wdkv", [D, 512])
    Wuq = c.dram_in("Wuq", [512, D])
    Wuk = c.dram_in("Wuk", [512, D])
    Wuv = c.dram_in("Wuv", [512, D])
    Wo = c.dram_in("Wo", [D, D])
    lnsc = c.dram_in("lnsc", [128, 8])
    lnbi = c.dram_in("lnbi", [128, 8])
    ctab = c.dram_in("ctab", [64, TF])
    stab = c.dram_in("stab", [64, TF])
    perm_in = c.dram_in("perm", [64, 64])
    ident_in = c.dram_in("ident", [128, 128])
    xout = c.dram_out("xout", [D, T])

    Xf = c.sb("Xf", [128, CD * T], F32)
    Cq = c.sb("Cq", [128, 4 * T], BF16)
    Ckv = c.sb("Ckv", [128, 4 * TF], BF16)
    psF = c.ps("psF", [128, TF])
    psA = psF[:, 0:T]
    psB = psF[:, T:2 * T]
    ones_col = c.sb("ones_col", [128, 1], BF16)
    ones_row = c.sb("ones_row", [1, 128], F32)
    identb = c.sb("identb", [128, 128], BF16)
    lnscs = c.sb("lnscs", [128, 8], F32)
    lnbis = c.sb("lnbis", [128, 8], F32)
    Wds = c.sb("Wds", [128, 2 * CD * 128], BF16)

    s_init = p.step("vector", lambda v: (v.memset(ones_col[:], 1.0),
                                         v.memset(ones_row[:], 1.0))[-1])
    s_cst = p.step("vector", lambda v: c.const_init_fn(v))
    s_ldw0 = p.dma("gpsimd", lambda e, ds: (
        e.dma_start(out=identb[:], in_=ident_in).then_inc(ds, 16),
        e.dma_start(out=lnscs[:], in_=lnsc).then_inc(ds, 16),
        e.dma_start(out=lnbis[:], in_=lnbi).then_inc(ds, 16))[-1], 3)

    stack2 = ExitStack()
    c2 = Ctx(nc, stack2)
    Xb = c2.sb("Xb", [128, CD * TF], BF16)
    LNS = c2.sb("LNS", [128, 4 * 512], BF16)
    lnrow0 = c2.sb("lnrow0", [1, T], F32)

    s_ldx = p.dma("sync", lambda e, ds: e.dma_start(
        out=Xf[:].rearrange("p (c t) -> p c t", t=T),
        in_=xloc.rearrange("(c p) t -> p c t", p=128)).then_inc(ds, 16), 1)
    s_ldo = p.dma("gpsimd", lambda e, ds: e.dma_start(
        out=Xb[:].rearrange("p (c t) -> p c t", t=TF)[:, :, T:2 * T],
        in_=xoth.rearrange("(c p) t -> p c t", p=128)).then_inc(ds, 16), 1)

    s_g1, lnrow = emit_ln(p, c, Xf, Xb, "g1", deps=[s_ldx, s_init, s_cst],
                          psA=psA, psB=psB, ones_col=ones_col, ones_row=ones_row,
                          scratch=LNS, xb_stride=TF, xb_off=0, row_buf=lnrow0)
    s_g2, _ = emit_ln(p, c, Xf, Xb, "g2", deps=[s_ldo, s_init, s_cst, s_g1],
                      psA=psA, psB=psB, ones_col=ones_col, ones_row=ones_row,
                      scratch=LNS, xb_stride=TF, xb_off=T, skip_cast=True,
                      row_buf=lnrow)
    def gadd(v):
        last = None
        for ci in range(CD):
            last = v.tensor_tensor(Xf[:, ci * T:(ci + 1) * T], Xf[:, ci * T:(ci + 1) * T],
                                   Xb[:, ci * TF: ci * TF + T], ALU.add)
        return last
    s_gadd = p.step("vector", gadd, deps=[s_g2])

    gam = [lnscs[:, i:i + 1] for i in range(8)]
    bet = [lnbis[:, i:i + 1] for i in range(8)]
    s_h1, _ = emit_ln(p, c, Xf, Xb, "h1", deps=[s_g2, s_ldw0, s_gadd], psA=psA, psB=psB,
                      ones_col=ones_col, ones_row=ones_row, scratch=LNS,
                      xb_stride=TF, xb_off=0, skip_cast=True, row_buf=lnrow,
                      affine=(gam, bet))
    s_h2, _ = emit_ln(p, c, Xf, Xb, "h2", deps=[s_h1], psA=psA, psB=psB,
                      ones_col=ones_col, ones_row=ones_row, scratch=LNS,
                      xb_stride=TF, xb_off=T, skip_cast=True, row_buf=lnrow,
                      affine=(gam, bet))
    s_h = s_h2

    if phase == 0:
        s_dmp = p.step("vector", lambda v: v.tensor_copy(Xf[:, 0:2 * T], Xb[:, 0:TF]),
                       deps=[s_h])
        stack2.pop_all() if False else None
        s_st = p.dma("sync", lambda e, ds: e.dma_start(
            out=xout.rearrange("(c p) t -> p c t", p=128),
            in_=Xf[:].rearrange("p (c t) -> p c t", t=T)).then_inc(ds, 16), 1,
            deps=[s_dmp])
        p.step("sync", lambda e: e.nop(), deps=[s_st])
        p.lower(nc, stack)
        stack.callback(stack2.close)
        return nc, stack
    proj_eps = []
    prev_mm = None
    prev_lw = None
    for wi, (wsrc, osb, ntok, ioff) in enumerate(
            ((Wdq, Cq, T, 0), (Wdkv, Ckv, TF, 0))):
        for m in range(4):
            pb = (wi * 4 + m) % 2
            def ldw(e, ds, wsrc=wsrc, m=m, pb=pb):
                return e.dma_start(
                    out=Wds[:, pb * 1024:pb * 1024 + 1024]
                    .rearrange("p (c f) -> p c f", f=128),
                    in_=wsrc[:, m * 128:(m + 1) * 128]
                    .rearrange("(c p) f -> p c f", p=128)).then_inc(ds, 16)
            dd = [prev_mm] if prev_mm is not None and (wi * 4 + m) >= 2 else [s_ldx]
            s_lw = p.dma("gpsimd", ldw, 1, deps=dd)
            for n in range(ntok // 512):
                def mm(t, m=m, n=n, pb=pb, ntok=ntok):
                    last = None
                    for ci in range(CD):
                        last = t.matmul(
                            psF[:, (n % 4) * 512:(n % 4) * 512 + 512],
                            Wds[:, pb * 1024 + ci * 128: pb * 1024 + ci * 128 + 128],
                            Xb[:, ci * TF + n * 512: ci * TF + n * 512 + 512],
                            start=(ci == 0), stop=(ci == CD - 1))
                    return last
                gi = len(proj_eps)
                s_mm = p.step("tensor", mm,
                              deps=[s_lw, s_h] + ([proj_eps[gi - 4]] if gi >= 4 else []))
                prev_mm = s_mm
                s_ep = p.step("vector", (lambda v, m=m, n=n, osb=osb, ntok=ntok:
                              v.tensor_copy(osb[:, m * ntok + n * 512: m * ntok + n * 512 + 512],
                                            psF[:, (n % 4) * 512:(n % 4) * 512 + 512])),
                              deps=[s_mm])
                proj_eps.append(s_ep)

    stack2.close()
    qs = c.sb("qs", [64, T], BF16)
    ks = c.sb("ks", [64, TF], BF16)
    swp = c.sb("swp", [64, TF], BF16)
    vtm = c.sb("vtm", [128, 32 * 64], BF16)
    probs = c.sb("probs", [128, TF], BF16)
    PT = c.sb("PT", [128, 16 * 128], BF16)
    avtmp = c.sb("avtmp", [64, T], BF16)
    AVb = c.sb("AVb", [128, CD * T], BF16)
    srow = c.sb("srow", [128, 2], F32)
    Ct = c.sb("Ct", [64, TF], BF16)
    St = c.sb("St", [64, TF], BF16)
    Wus = c.sb("Wus", [128, 3 * 4 * 64], BF16)
    permb = c.sb("permb", [64, 64], BF16)
    Wos = Wds[:, 0:CD * 128]

    s_ldt = p.dma("gpsimd", lambda e, ds: (
        e.dma_start(out=Ct[:], in_=ctab).then_inc(ds, 16),
        e.dma_start(out=St[:], in_=stab).then_inc(ds, 16),
        e.dma_start(out=permb[:], in_=perm_in).then_inc(ds, 16))[-1], 3,
        deps=proj_eps[-1:])

    out_eps_all = []
    prev_head_tail = [s_h] + proj_eps[-1:]
    for h in range(16 if phase != 4 else 1):
        def ld_wh(g, ds, h=h):
            g.dma_start(out=Wus[:, 0:256].rearrange("p (c f) -> p c f", f=64),
                        in_=Wuq[:, h * 64:(h + 1) * 64]
                        .rearrange("(c p) f -> p c f", p=128)).then_inc(ds, 16)
            g.dma_start(out=Wus[:, 256:512].rearrange("p (c f) -> p c f", f=64),
                        in_=Wuk[:, h * 64:(h + 1) * 64]
                        .rearrange("(c p) f -> p c f", p=128)).then_inc(ds, 16)
            return g.dma_start(out=Wus[:, 512:768].rearrange("p (c f) -> p c f", f=64),
                               in_=Wuv[:, h * 64:(h + 1) * 64]
                               .rearrange("(c p) f -> p c f", p=128)).then_inc(ds, 16)
        s_ldh = p.dma("gpsimd", ld_wh, 3, deps=prev_head_tail)

        q_eps = []
        for n in range(NT):
            bk = (n % 2) * 512
            def mmq(t, n=n, bk=bk):
                last = None
                for ci in range(4):
                    last = t.matmul(psF[0:64, bk:bk + 512],
                                    Wus[:, ci * 64:(ci + 1) * 64],
                                    Cq[:, ci * T + n * 512: ci * T + n * 512 + 512],
                                    start=(ci == 0), stop=(ci == 3))
                return last
            s_mm = p.step("tensor", mmq,
                          deps=[s_ldh] + ([q_eps[n - 2]] if n >= 2 else prev_head_tail))
            s_ep = p.step("scalar", (lambda s, n=n, bk=bk: s.activation(
                out=qs[:, n * 512:(n + 1) * 512], in_=psF[0:64, bk:bk + 512],
                func=AF.Identity, bias=0.0, scale=1.0)), deps=[s_mm])
            q_eps.append(s_ep)
        def mm_swq(t):
            last = None
            for n in range(NT):
                last = t.matmul(psF[0:64, n * 512:(n + 1) * 512], permb[:],
                                qs[:, n * 512:(n + 1) * 512], start=True, stop=True)
            return last
        s_sw = p.step("tensor", mm_swq, deps=q_eps + [s_ldt])
        s_r1 = p.step("vector", lambda v: v.tensor_tensor(
            qs[:], qs[:], Ct[:, 0:T], ALU.mult), deps=[s_sw, s_ldt])
        s_r2 = p.step("gpsimd", lambda g: g.tensor_tensor(
            swp[0:64, 0:T], Ct[0:64, 0:T] if False else St[:, 0:T], St[:, 0:T],
            ALU.mult), deps=[s_sw]) if False else None
        s_r2 = p.step("vector", lambda v: v.tensor_tensor(
            swp[0:64, 0:T], psF[0:64, 0:T], St[:, 0:T], ALU.mult), deps=[s_sw, s_r1])
        s_r3 = p.step("vector", lambda v: v.tensor_tensor(
            qs[:], qs[:], swp[0:64, 0:T], ALU.add),
            deps=[s_r1, s_r2])

        k_eps = []
        for n in range(TF // 512):
            bk = 1024 + (n % 2) * 512
            def mmk(t, n=n, bk=bk):
                last = None
                for ci in range(4):
                    last = t.matmul(psF[0:64, bk:bk + 512],
                                    Wus[:, 256 + ci * 64: 256 + (ci + 1) * 64],
                                    Ckv[:, ci * TF + n * 512: ci * TF + n * 512 + 512],
                                    start=(ci == 0), stop=(ci == 3))
                return last
            s_mm = p.step("tensor", mmk,
                          deps=[s_ldh, s_r1, s_r2] + ([k_eps[n - 2]] if n >= 2 else prev_head_tail))
            s_ep = p.step("scalar", (lambda s, n=n, bk=bk: s.activation(
                out=ks[:, n * 512:(n + 1) * 512], in_=psF[0:64, bk:bk + 512],
                func=AF.Identity, bias=0.0, scale=1.0)), deps=[s_mm])
            k_eps.append(s_ep)
        def mm_swk(t):
            last = None
            for n in range(TF // 512):
                last = t.matmul(psF[0:64, n * 512:(n + 1) * 512], permb[:],
                                ks[:, n * 512:(n + 1) * 512], start=True, stop=True)
            return last
        s_sk = p.step("tensor", mm_swk, deps=k_eps + [s_r2, s_r3])
        s_k1 = p.step("vector", lambda v: v.tensor_tensor(
            ks[:], ks[:], Ct[:], ALU.mult), deps=k_eps + [s_r3, s_sk])
        s_k2 = p.step("vector", lambda v: v.tensor_tensor(
            swp[:], psF[0:64, 0:TF], St[:], ALU.mult), deps=[s_sk, s_k1])
        s_k3 = p.step("vector", lambda v: v.tensor_tensor(
            ks[:], ks[:], swp[:], ALU.add), deps=[s_k1, s_k2])

        v_eps = []
        for g2 in range(32):
            bk = (4 + g2 % 4) * 512
            def mmv(t, g2=g2, bk=bk):
                last = None
                for ci in range(4):
                    last = t.matmul(psF[:, bk:bk + 64],
                                    Ckv[:, ci * TF + g2 * 128: ci * TF + g2 * 128 + 128],
                                    Wus[:, 512 + ci * 64: 512 + (ci + 1) * 64],
                                    start=(ci == 0), stop=(ci == 3))
                return last
            s_mm = p.step("tensor", mmv,
                          deps=[s_ldh, s_k2] + ([v_eps[g2 - 4]] if g2 >= 4 else prev_head_tail))
            s_ep = p.step("vector", (lambda v, g2=g2, bk=bk: v.tensor_copy(
                vtm[:, g2 * 64:(g2 + 1) * 64], psF[:, bk:bk + 64])),
                deps=[s_mm])
            v_eps.append(s_ep)

        head_tail = []
        prev_sc = None
        for qt in range(16):
            def mmsc(t, qt=qt):
                last = None
                for kc in range(8):
                    last = t.matmul(psF[:, kc * 512:(kc + 1) * 512],
                                    qs[:, qt * 128:(qt + 1) * 128],
                                    ks[:, kc * 512:(kc + 1) * 512],
                                    start=True, stop=True)
                return last
            dd = [s_r3, s_k3] + v_eps[-4:]
            if prev_sc is not None:
                dd = dd + [prev_sc]
            s_sc = p.step("tensor", mmsc, deps=dd)
            s_mx = p.step("vector", (lambda v: v.tensor_reduce(
                srow[:, 0:1], psF[:, 0:TF], AX.X, ALU.max, negate=True)), deps=[s_sc])
            s_mxs = p.step("gpsimd", (lambda g: g.tensor_scalar_mul(
                srow[:, 0:1], srow[:, 0:1], 0.125)), deps=[s_mx])
            s_ex = p.step("scalar", (lambda s: s.activation(
                out=probs[:], in_=psF[:, 0:TF], func=AF.Exp, bias=srow[:, 0:1],
                scale=0.125, accum_out=srow[:, 1:2])), deps=[s_mxs])
            s_rc = p.step("vector", lambda v: v.reciprocal(srow[:, 1:2], srow[:, 1:2]),
                          deps=[s_ex])
            s_nm = p.step("vector", (lambda v: v.tensor_scalar(
                probs[:], probs[:], srow[:, 1:2], None, ALU.mult)), deps=[s_rc])
            s_av = None
            s_ae0 = None
            prev_ev = None
            for wv in range(2):
                ev_steps = []
                for gr in range(2):
                    def trg(t, wv=wv, gr=gr):
                        last = None
                        for j in range(8):
                            last = t.transpose(
                                psF.bitcast(BF16)[:, j * 1024 + 256: j * 1024 + 384],
                                probs[:, (wv * 16 + gr * 8 + j) * 128:
                                      (wv * 16 + gr * 8 + j + 1) * 128],
                                identb[:])
                        return last
                    dd2 = [s_nm] + ([prev_ev] if prev_ev is not None else []) + \
                          ([s_av] if s_av is not None else []) + \
                          ([s_ae0] if (wv == 1 and s_av is not None) else [])
                    s_tr = p.step("tensor", trg, deps=dd2)
                    def evg(v, gr=gr):
                        last = None
                        for j in range(8):
                            last = v.tensor_copy(
                                PT[:, (gr * 8 + j) * 128: (gr * 8 + j + 1) * 128],
                                psF.bitcast(BF16)[:, j * 1024 + 256: j * 1024 + 384])
                        return last
                    s_ev = p.step("vector", evg, deps=[s_tr])
                    prev_ev = s_ev
                    ev_steps.append(s_ev)
                def mmav(t, wv=wv):
                    last = None
                    for g2 in range(16):
                        last = t.matmul(psF[0:64, 0:128],
                                        vtm[:, (wv * 16 + g2) * 64:(wv * 16 + g2 + 1) * 64],
                                        PT[:, g2 * 128:(g2 + 1) * 128],
                                        start=(g2 == 0), stop=(g2 == 15))
                    return last
                s_av = p.step("tensor", mmav, deps=ev_steps + ([s_ae0] if wv else []))
                if wv == 0:
                    s_ae0 = p.step("vector", (lambda v, qt=qt: v.tensor_copy(
                        avtmp[:, qt * 128:(qt + 1) * 128], psF[0:64, 0:128])),
                        deps=[s_av])
            s_ae = p.step("vector", (lambda v, qt=qt: v.tensor_tensor(
                avtmp[:, qt * 128:(qt + 1) * 128], avtmp[:, qt * 128:(qt + 1) * 128],
                psF[0:64, 0:128], ALU.add)), deps=[s_av, s_ae0])
            prev_sc = s_ae
            head_tail.append(s_ae)
        s_mv = p.dma("sync", (lambda e, ds, h=h: e.dma_start(
            out=AVb[(h % 2) * 64:(h % 2) * 64 + 64, (h // 2) * T:(h // 2 + 1) * T],
            in_=avtmp[:]).then_inc(ds, 16)), 1, deps=head_tail[-1:])
        prev_head_tail = [s_mv]
        out_eps_all.append(s_mv)
        if phase == 4:
            def dumps(v):
                v.tensor_copy(Xf[0:64, 0:T], qs[:, 0:T])
                v.tensor_copy(Xf[0:64, T:3 * T], ks[:, 0:TF])
                v.tensor_copy(Xf[:, 3 * T:4 * T], Cq[:, 0:T])
                v.tensor_copy(Xf[0:64, 4 * T:5 * T], avtmp[:, 0:T])
                return v.tensor_copy(Xf[:, 5 * T:7 * T], Ckv[:, 0:TF])
            s_cpd = p.step("vector", dumps, deps=out_eps_all)
            s_st = p.dma("sync", lambda e, ds: e.dma_start(
                out=xout.rearrange("(c p) t -> p c t", p=128),
                in_=Xf[:].rearrange("p (c t) -> p c t", t=T)).then_inc(ds, 16), 1,
                deps=[s_cpd])
            p.step("sync", lambda e: e.nop(), deps=[s_st])
            p.lower(nc, stack)
            return nc, stack

    if phase == 5:
        s_cpd = p.step("vector", lambda v: v.tensor_copy(Xf[:], AVb[:]),
                       deps=out_eps_all)
        s_st = p.dma("sync", lambda e, ds: e.dma_start(
            out=xout.rearrange("(c p) t -> p c t", p=128),
            in_=Xf[:].rearrange("p (c t) -> p c t", t=T)).then_inc(ds, 16), 1,
            deps=[s_cpd])
        p.step("sync", lambda e: e.nop(), deps=[s_st])
        p.lower(nc, stack)
        return nc, stack

    prev_lw = None
    prev_mm_o = None
    oeps = []
    for m in range(CD):
        def ldwp(e, ds, m=m):
            return e.dma_start(out=Wos[:].rearrange("p (c f) -> p c f", f=128),
                               in_=Wo[:, m * 128:(m + 1) * 128]
                               .rearrange("(c p) f -> p c f", p=128)).then_inc(ds, 16)
        s_lw = p.dma("gpsimd", ldwp, 1,
                     deps=([prev_mm_o] if prev_lw is not None else out_eps_all))
        prev_lw = s_lw
        for n in range(NT):
            def mmo(t, m=m, n=n):
                last = None
                for ci in range(CD):
                    last = t.matmul(
                        psF[:, ((m * NT + n) % 4) * 512: ((m * NT + n) % 4) * 512 + 512],
                        Wds[:, ci * 128:(ci + 1) * 128],
                        AVb[:, ci * T + n * 512: ci * T + n * 512 + 512],
                        start=(ci == 0), stop=(ci == CD - 1))
                return last
            gi = m * NT + n
            s_mm = p.step("tensor", mmo,
                          deps=[s_lw] + out_eps_all[-1:] + ([oeps[gi - 4]] if gi >= 4 else []))
            prev_mm_o = s_mm
            def epo(v, m=m, n=n):
                sl = slice(m * T + n * 512, m * T + n * 512 + 512)
                if phase == 7:
                    return v.tensor_copy(
                        Xf[:, sl],
                        psF[:, ((m * NT + n) % 4) * 512: ((m * NT + n) % 4) * 512 + 512])
                return v.tensor_tensor(
                    Xf[:, sl],
                    psF[:, ((m * NT + n) % 4) * 512: ((m * NT + n) % 4) * 512 + 512],
                    Xf[:, sl], ALU.add)
            s_ep = p.step("vector", epo, deps=[s_mm])
            oeps.append(s_ep)

    if phase == 6:
        def dmp6(v):
            v.tensor_copy(Xf[:, 0:1024], Wos)
            return v.tensor_copy(Xf[:, 2048:4096], AVb[:, 0:2048])
        s_d6 = p.step("vector", dmp6, deps=oeps)
        oeps = oeps + [s_d6]
    s_st = p.dma("sync", lambda e, ds: e.dma_start(
        out=xout.rearrange("(c p) t -> p c t", p=128),
        in_=Xf[:].rearrange("p (c t) -> p c t", t=T)).then_inc(ds, 16), 1, deps=oeps)
    p.step("sync", lambda e: e.nop(), deps=[s_st])
    p.lower(nc, stack)
    return nc, stack


def _rope_tables_L0_host(seq_len=TF, dim=64, base=10000.0):
    inv_freq = 1.0 / base ** (np.arange(0, dim, 2, dtype=np.float32) / dim)
    t = np.arange(seq_len, dtype=np.float32)
    freqs = np.outer(t, inv_freq)
    cos = np.concatenate([np.cos(freqs), np.cos(freqs)], -1)[:, ::2]
    sin = np.concatenate([np.sin(freqs), np.sin(freqs)], -1)[:, ::2]
    C = np.empty((dim, seq_len), np.float32)
    S = np.empty((dim, seq_len), np.float32)
    C[0::2] = cos.T
    C[1::2] = cos.T
    S[0::2] = -sin.T
    S[1::2] = sin.T
    return C, S


def a0_inputs(x_fm_cores, ins):
    Wdq = np.ascontiguousarray(np.asarray(ins["L0_attn_w_down_q"], np.float32).T)
    Wdkv = np.ascontiguousarray(np.asarray(ins["L0_attn_w_down_kv"], np.float32).T)
    Wuq = np.ascontiguousarray(np.asarray(ins["L0_attn_w_up_q"], np.float32).T)
    Wuk = np.ascontiguousarray(np.asarray(ins["L0_attn_w_up_k"], np.float32).T)
    Wuv = np.ascontiguousarray(np.asarray(ins["L0_attn_w_up_v"], np.float32).T)
    Wo = np.ascontiguousarray(np.asarray(ins["L0_attn_w_out"], np.float32).T)
    lnsc = _bias_tile(np.asarray(ins["L0_attn_ln_scale"], np.float32))
    lnbi = _bias_tile(np.asarray(ins["L0_attn_ln_bias"], np.float32))
    C, S = _rope_tables_L0_host()
    ident = np.eye(128, dtype=np.float32)
    perm = np.zeros((64, 64), np.float32)
    for i in range(32):
        perm[2 * i, 2 * i + 1] = 1.0
        perm[2 * i + 1, 2 * i] = 1.0
    out = []
    for cc in range(NCORES):
        hh = cc % 2
        idx0 = slice(hh * T, (hh + 1) * T)
        idx1 = slice((1 - hh) * T, (2 - hh) * T)
        Cc = np.ascontiguousarray(np.concatenate([C[:, idx0], C[:, idx1]], axis=1))
        Sc = np.ascontiguousarray(np.concatenate([S[:, idx0], S[:, idx1]], axis=1))
        out.append(dict(xloc=x_fm_cores[cc], xoth=x_fm_cores[cc ^ 1],
                        Wdq=Wdq, Wdkv=Wdkv, Wuq=Wuq, Wuk=Wuk, Wuv=Wuv, Wo=Wo,
                        lnsc=lnsc, lnbi=lnbi, ctab=Cc, stab=Sc, ident=ident,
                        perm=perm))
    return out


def build_a3():
    nc = bass.Bass("TRN2", target_bir_lowering=False)
    stack = ExitStack()
    c = Ctx(nc, stack)
    p = P()
    c.setup_consts([LN_EPS, 0.0])

    xloc = c.dram_in("xloc", [D, T])
    xoth = c.dram_in("xoth", [D, T])
    Wa = c.dram_in("Wa", [D, 2 * D])
    dmask_in = c.dram_in("dmask", [4, 128, 512])
    crole_in = c.dram_in("crole", [128, 1])
    flag_in = c.dram_in("flag", [128, 1])
    rtab_in = c.dram_in("rtab", [128, 16])
    abg_in = c.dram_in("abg", [1, 3])
    triu_in = c.dram_in("triu", [128, 128])
    ident_in = c.dram_in("ident", [128, 128])
    ident2_in = c.dram_in("ident2", [128, 64])
    xout = c.dram_out("xout", [D, T])

    Xf = c.sb("Xf", [128, CD * T], F32)
    Xb = c.sb("Xb", [128, CD * TF], BF16)
    psF = c.ps("psF", [128, TF])
    psA = psF[:, 0:T]
    psB = psF[:, T:2 * T]
    ones_col = c.sb("ones_col", [128, 1], BF16)
    ones_row = c.sb("ones_row", [1, 128], F32)
    identb = c.sb("identb", [128, 128], BF16)
    ident2b = c.sb("ident2b", [128, 64], BF16)
    identf = c.sb("identf", [128, 128], F32)
    triub = c.sb("triub", [128, 128], BF16)
    dmaskb = c.sb("dmaskb", [128, 4 * 512], BF16)
    croleb = c.sb("croleb", [128, 1], F32)
    flagb = c.sb("flagb", [128, 1], F32)
    rtabb = c.sb("rtabb", [128, 16], F32)
    abgc = c.sb("abgc", [128, 4], F32)
    abgr = c.sb("abgr", [1, 3], F32)
    AVb = c.sb("AVb", [128, CD * T], BF16)
    srow = c.sb("srow", [128, 4], F32)
    offs = c.sb("offs", [1, 128], F32)
    otht = c.sb("otht", [128, 1], F32)
    Was = c.sb("Was", [128, CD * 64], BF16)
    stack3 = ExitStack()
    c3 = Ctx(nc, stack3)
    lnrow0 = c3.sb("lnrow0", [1, T], F32)

    s_init = p.step("vector", lambda v: (v.memset(ones_col[:], 1.0),
                                         v.memset(ones_row[:], 1.0))[-1])
    s_cst = p.step("vector", lambda v: c.const_init_fn(v))
    s_ldc = p.dma("gpsimd", lambda e, ds: (
        e.dma_start(out=identb[:], in_=ident_in).then_inc(ds, 16),
        e.dma_start(out=identf[:], in_=ident_in).then_inc(ds, 16),
        e.dma_start(out=ident2b[:], in_=ident2_in).then_inc(ds, 16),
        e.dma_start(out=triub[:], in_=triu_in).then_inc(ds, 16),
        e.dma_start(out=dmaskb[:].rearrange("p (a f) -> p a f", f=512),
                    in_=dmask_in.rearrange("a p f -> p a f")).then_inc(ds, 16),
        e.dma_start(out=croleb[:], in_=crole_in).then_inc(ds, 16),
        e.dma_start(out=flagb[:], in_=flag_in).then_inc(ds, 16),
        e.dma_start(out=rtabb[:], in_=rtab_in).then_inc(ds, 16),
        e.dma_start(out=abgr[:], in_=abg_in).then_inc(ds, 16))[-1], 9)

    s_ldx = p.dma("sync", lambda e, ds: e.dma_start(
        out=Xf[:].rearrange("p (c t) -> p c t", t=T),
        in_=xloc.rearrange("(c p) t -> p c t", p=128)).then_inc(ds, 16), 1)
    s_ldo = p.dma("gpsimd", lambda e, ds: e.dma_start(
        out=Xb[:].rearrange("p (c t) -> p c t", t=TF)[:, :, T:2 * T],
        in_=xoth.rearrange("(c p) t -> p c t", p=128)).then_inc(ds, 16), 1)

    s_h1, lnrow = emit_ln(p, c, Xf, Xb, "lnl", deps=[s_ldx, s_init, s_cst],
                          psA=psA, psB=psB, ones_col=ones_col, ones_row=ones_row,
                          scratch=AVb, xb_stride=TF, xb_off=0, row_buf=lnrow0)
    s_h2, _ = emit_ln(p, c, Xf, Xb, "lno", deps=[s_ldo, s_init, s_cst, s_h1],
                      psA=psA, psB=psB, ones_col=ones_col, ones_row=ones_row,
                      scratch=AVb, xb_stride=TF, xb_off=T, skip_cast=True,
                      row_buf=lnrow)
    s_h = s_h2
    stack3.close()
    qs = c.sb("qs", [64, T], BF16)
    ks = c.sb("ks", [64, TF], BF16)
    vtm = c.sb("vtm", [128, 32 * 64], BF16)
    probs = c.sb("probs", [128, TF], BF16)
    PT = c.sb("PT", [128, 16 * 128], BF16)
    avtmp = c.sb("avtmp", [64, T], BF16)
    Vtm = c.sb("Vtm", [128, 16 * 128], BF16)
    PmTm = c.sb("PmTm", [128, 128], BF16)
    Pm = c.sb("Pm", [128, T], BF16)

    s_ab1 = p.step("tensor", lambda t: t.matmul(
        psF[:, 0:3], ones_row[:], abgr[:], start=True, stop=True),
        deps=[s_ldc, s_h])
    s_ab2 = p.step("vector", lambda v: v.tensor_copy(abgc[:, 0:3], psF[:, 0:3]),
                   deps=[s_ab1])
    s_ab3 = p.step("gpsimd", lambda g: g.tensor_scalar_mul(
        abgc[:, 3:4], abgc[:, 2:3], -1.0), deps=[s_ab2])

    out_eps_all = []
    prev_head_tail = [s_h, s_ab3]
    for h in range(16):
        def ld_wh(g, ds, h=h):
            return g.dma_start(out=Was[:].rearrange("p (c f) -> p c f", f=64),
                               in_=Wa[:, h * 64:(h + 1) * 64]
                               .rearrange("(c p) f -> p c f", p=128)).then_inc(ds, 16)
        s_ldh = p.dma("gpsimd", ld_wh, 1, deps=prev_head_tail)

        q_eps = []
        for n in range(NT):
            bk = (n % 2) * 512
            def mmq(t, n=n, bk=bk):
                last = None
                for ci in range(CD):
                    last = t.matmul(psF[0:64, bk:bk + 512],
                                    Was[:, ci * 64:(ci + 1) * 64],
                                    Xb[:, ci * TF + n * 512: ci * TF + n * 512 + 512],
                                    start=(ci == 0), stop=(ci == CD - 1))
                return last
            s_mm = p.step("tensor", mmq,
                          deps=[s_ldh] + ([q_eps[n - 2]] if n >= 2 else prev_head_tail))
            s_ep = p.step("scalar", (lambda s, n=n, bk=bk: s.activation(
                out=qs[:, n * 512:(n + 1) * 512], in_=psF[0:64, bk:bk + 512],
                func=AF.Identity, bias=0.0, scale=1.0)), deps=[s_mm])
            q_eps.append(s_ep)

        def ld_wk(g, ds, h=h):
            return g.dma_start(out=Was[:].rearrange("p (c f) -> p c f", f=64),
                               in_=Wa[:, D + h * 64: D + (h + 1) * 64]
                               .rearrange("(c p) f -> p c f", p=128)).then_inc(ds, 16)
        s_ldk = p.dma("gpsimd", ld_wk, 1, deps=q_eps)
        k_eps = []
        for n in range(TF // 512):
            bk = 1024 + (n % 2) * 512
            def mmk(t, n=n, bk=bk):
                last = None
                for ci in range(CD):
                    last = t.matmul(psF[0:64, bk:bk + 512],
                                    Was[:, ci * 64:(ci + 1) * 64],
                                    Xb[:, ci * TF + n * 512: ci * TF + n * 512 + 512],
                                    start=(ci == 0), stop=(ci == CD - 1))
                return last
            s_mm = p.step("tensor", mmk,
                          deps=[s_ldk] + ([k_eps[n - 2]] if n >= 2 else []))
            s_ep = p.step("scalar", (lambda s, n=n, bk=bk: s.activation(
                out=ks[:, n * 512:(n + 1) * 512], in_=psF[0:64, bk:bk + 512],
                func=AF.Identity, bias=0.0, scale=1.0)), deps=[s_mm])
            k_eps.append(s_ep)

        hb = (h % 2) * 64
        v_eps = []
        prev_ev = None
        for g0 in range(0, 32, 4):
            def trv(t, g0=g0, h=h, hb=hb):
                last = None
                for j in range(4):
                    g2 = g0 + j
                    last = t.transpose(
                        psF.bitcast(BF16)[:, j * 1024: j * 1024 + 64],
                        Xb[hb:hb + 64, (h // 2) * TF + g2 * 128:
                           (h // 2) * TF + g2 * 128 + 128],
                        ident2b[hb:hb + 64, :])
                return last
            dd = [s_h, s_ldc] + ([prev_ev] if prev_ev is not None else prev_head_tail)
            s_tr = p.step("tensor", trv, deps=dd + k_eps[-2:])
            def evv(v, g0=g0):
                last = None
                for j in range(4):
                    last = v.tensor_copy(vtm[:, (g0 + j) * 64:(g0 + j + 1) * 64],
                                         psF.bitcast(BF16)[:, j * 1024: j * 1024 + 64])
                return last
            s_ev = p.step("vector", evv, deps=[s_tr])
            prev_ev = s_ev
            v_eps.append(s_ev)

        head_tail = []
        prev_sc = None
        for qt in range(16):
            d = qt // 4
            def mmsc(t, qt=qt):
                last = None
                for kc in range(8):
                    last = t.matmul(psF[:, kc * 512:(kc + 1) * 512],
                                    qs[:, qt * 128:(qt + 1) * 128],
                                    ks[:, kc * 512:(kc + 1) * 512],
                                    start=True, stop=True)
                return last
            dd = q_eps[-2:] + k_eps[-2:] + v_eps[-1:]
            if prev_sc is not None:
                dd = dd + [prev_sc]
            s_sc = p.step("tensor", mmsc, deps=dd)
            def cmsk(v, qt=qt, d=d):
                v.tensor_tensor(psF[:, d * 512:(d + 1) * 512],
                                psF[:, d * 512:(d + 1) * 512],
                                dmaskb[:, (qt % 4) * 512:(qt % 4) * 512 + 512],
                                ALU.add)
                if d < 3:
                    v.tensor_single_scalar(psF[:, (d + 1) * 512: 4 * 512],
                                           psF[:, (d + 1) * 512: 4 * 512],
                                           -1e30, ALU.add)
                return v.tensor_scalar(psF[:, 4 * 512: 8 * 512],
                                       psF[:, 4 * 512: 8 * 512],
                                       croleb[:, 0:1], None, ALU.add)
            s_mk = p.step("vector", cmsk, deps=[s_sc, s_ldc])
            s_mx = p.step("vector", (lambda v: v.tensor_reduce(
                srow[:, 0:1], psF[:, 0:TF], AX.X, ALU.max, negate=True)), deps=[s_mk])
            s_mxs = p.step("gpsimd", (lambda g: g.tensor_scalar_mul(
                srow[:, 0:1], srow[:, 0:1], 0.125)), deps=[s_mx])
            s_ex = p.step("scalar", (lambda s: s.activation(
                out=probs[:], in_=psF[:, 0:TF], func=AF.Exp, bias=srow[:, 0:1],
                scale=0.125, accum_out=srow[:, 1:2])), deps=[s_mxs])
            s_rc = p.step("vector", lambda v: v.reciprocal(srow[:, 1:2], srow[:, 1:2]),
                          deps=[s_ex])
            s_nm = p.step("vector", (lambda v: v.tensor_scalar(
                probs[:], probs[:], srow[:, 1:2], None, ALU.mult)), deps=[s_rc])
            s_av = None
            s_ae0 = None
            prev_ev2 = None
            for wv in range(2):
                ev_steps = []
                for gr in range(2):
                    def trg(t, wv=wv, gr=gr):
                        last = None
                        for j in range(8):
                            last = t.transpose(
                                psF.bitcast(BF16)[:, j * 1024 + 256: j * 1024 + 384],
                                probs[:, (wv * 16 + gr * 8 + j) * 128:
                                      (wv * 16 + gr * 8 + j + 1) * 128],
                                identb[:])
                        return last
                    dd2 = [s_nm] + ([prev_ev2] if prev_ev2 is not None else []) + \
                          ([s_av] if s_av is not None else []) + \
                          ([s_ae0] if (wv == 1 and s_av is not None) else [])
                    s_tr = p.step("tensor", trg, deps=dd2)
                    def evg(v, gr=gr):
                        last = None
                        for j in range(8):
                            last = v.tensor_copy(
                                PT[:, (gr * 8 + j) * 128: (gr * 8 + j + 1) * 128],
                                psF.bitcast(BF16)[:, j * 1024 + 256: j * 1024 + 384])
                        return last
                    s_ev = p.step("vector", evg, deps=[s_tr])
                    prev_ev2 = s_ev
                    ev_steps.append(s_ev)
                def mmav(t, wv=wv):
                    last = None
                    for g2 in range(16):
                        last = t.matmul(psF[0:64, 0:128],
                                        vtm[:, (wv * 16 + g2) * 64:(wv * 16 + g2 + 1) * 64],
                                        PT[:, g2 * 128:(g2 + 1) * 128],
                                        start=(g2 == 0), stop=(g2 == 15))
                    return last
                s_av = p.step("tensor", mmav, deps=ev_steps + ([s_ae0] if wv else []))
                if wv == 0:
                    s_ae0 = p.step("vector", (lambda v, qt=qt: v.tensor_copy(
                        avtmp[:, qt * 128:(qt + 1) * 128], psF[0:64, 0:128])),
                        deps=[s_av])
            s_ae = p.step("vector", (lambda v, qt=qt: v.tensor_tensor(
                avtmp[:, qt * 128:(qt + 1) * 128], avtmp[:, qt * 128:(qt + 1) * 128],
                psF[0:64, 0:128], ALU.add)), deps=[s_av, s_ae0])
            prev_sc = s_ae
            head_tail.append(s_ae)
        s_mv = p.dma("sync", (lambda e, ds, h=h: e.dma_start(
            out=AVb[(h % 2) * 64:(h % 2) * 64 + 64, (h // 2) * T:(h // 2 + 1) * T],
            in_=avtmp[:]).then_inc(ds, 16)), 1, deps=head_tail[-1:])
        prev_head_tail = [s_mv]
        out_eps_all.append(s_mv)

    comb1 = []
    for ci in range(CD):
        s_c1 = p.step("vector", (lambda v, ci=ci: v.scalar_tensor_tensor(
            Xf[:, ci * T:(ci + 1) * T], AVb[:, ci * T:(ci + 1) * T],
            abgc[:, 1:2], Xf[:, ci * T:(ci + 1) * T], ALU.mult, ALU.add)),
            deps=out_eps_all + [s_ab3])
        s_c2 = p.step("vector", (lambda v, ci=ci: v.scalar_tensor_tensor(
            Xf[:, ci * T:(ci + 1) * T], Xb[:, ci * TF: ci * TF + T],
            abgc[:, 0:1], Xf[:, ci * T:(ci + 1) * T], ALU.mult, ALU.add)),
            deps=[s_c1])
        comb1.append(s_c2)

    pm_tail = list(comb1)
    for ci in range(CD):
        s_ot = p.step("vector", (lambda v, ci=ci: v.tensor_reduce(
            otht[:, 0:1], Xb[:, ci * TF + T: ci * TF + 2 * T], AX.X, ALU.add)),
            deps=pm_tail[-1:] + [s_h])
        s_of = p.step("gpsimd", lambda g: g.tensor_tensor(
            otht[:, 0:1], otht[:, 0:1], flagb[:, 0:1], ALU.mult), deps=[s_ot])
        s_tt = p.step("tensor", lambda t: t.transpose(
            psF[0:1, 3072:3072 + 128], otht[:, 0:1], identf[:]), deps=[s_of])
        s_oe = p.step("vector", lambda v: v.tensor_copy(offs[:], psF[0:1, 3072:3072 + 128]),
                      deps=[s_tt])
        prev_t = s_oe
        prev_ev3 = None
        vt_eps = []
        for g0 in range(0, 16, 4):
            def trv2(t, g0=g0, ci=ci):
                last = None
                for j in range(4):
                    g2 = g0 + j
                    last = t.transpose(
                        psF.bitcast(BF16)[:, j * 1024 + 512: j * 1024 + 640],
                        Xb[:, ci * TF + g2 * 128: ci * TF + g2 * 128 + 128],
                        identb[:])
                return last
            dd = [s_oe] + ([prev_ev3] if prev_ev3 is not None else [])
            s_t3 = p.step("tensor", trv2, deps=dd)
            def ev3(v, g0=g0):
                last = None
                for j in range(4):
                    last = v.tensor_copy(Vtm[:, (g0 + j) * 128:(g0 + j + 1) * 128],
                                         psF.bitcast(BF16)[:, j * 1024 + 512: j * 1024 + 640])
                return last
            s_e3 = p.step("vector", ev3, deps=[s_t3])
            prev_ev3 = s_e3
            vt_eps.append(s_e3)
        prev_cm = vt_eps[-1]
        offsb = c
        for g2 in range(16):
            def mmcs(t, g2=g2):
                t.matmul(psF[:, 2048 + (g2 % 2) * 512: 2048 + (g2 % 2) * 512 + 128],
                         triub[:], Vtm[:, g2 * 128:(g2 + 1) * 128],
                         start=True, stop=False)
                r = t.matmul(psF[:, 2048 + (g2 % 2) * 512: 2048 + (g2 % 2) * 512 + 128],
                             ones_row[:], offs[:], start=False, stop=True)
                return t.matmul(psF[0:1, 3072 + 512:3072 + 512 + 128], ones_col[:],
                                Vtm[:, g2 * 128:(g2 + 1) * 128], start=True, stop=True)
            s_cs = p.step("tensor", mmcs, deps=[prev_cm])
            s_ou = p.step("gpsimd", lambda g: g.engine_nop(), deps=[]) if False else None
            s_ou = p.step("vector", lambda v: v.tensor_tensor(
                offs[:], offs[:], psF[0:1, 3072 + 512:3072 + 512 + 128], ALU.add),
                deps=[s_cs])
            s_pe = p.step("vector", (lambda v, g2=g2: v.tensor_scalar(
                PmTm[:], psF[:, 2048 + (g2 % 2) * 512: 2048 + (g2 % 2) * 512 + 128],
                rtabb[:, g2:g2 + 1], None, ALU.mult)), deps=[s_cs, s_ou])
            s_pt = p.step("tensor", (lambda t, g2=g2: t.transpose(
                psF.bitcast(BF16)[:, 7 * 1024 + 512: 7 * 1024 + 640], PmTm[:],
                identb[:])), deps=[s_pe])
            s_pv = p.step("vector", (lambda v, g2=g2: v.tensor_copy(
                Pm[:, g2 * 128:(g2 + 1) * 128],
                psF.bitcast(BF16)[:, 7 * 1024 + 512: 7 * 1024 + 640])), deps=[s_pt])
            prev_cm = s_pv
        s_c3 = p.step("vector", (lambda v, ci=ci: v.scalar_tensor_tensor(
            Xf[:, ci * T:(ci + 1) * T], Pm[:], abgc[:, 3:4],
            Xf[:, ci * T:(ci + 1) * T], ALU.mult, ALU.add)), deps=[prev_cm, s_ab3])
        pm_tail.append(s_c3)

    s_st = p.dma("sync", lambda e, ds: e.dma_start(
        out=xout.rearrange("(c p) t -> p c t", p=128),
        in_=Xf[:].rearrange("p (c t) -> p c t", t=T)).then_inc(ds, 16), 1,
        deps=pm_tail)
    p.step("sync", lambda e: e.nop(), deps=[s_st])
    p.lower(nc, stack)
    return nc, stack


def a3_inputs(x_fm_cores, ins):
    Wa = np.ascontiguousarray(np.asarray(ins["L3_attn_W_attn"], np.float32).T)
    dmask = np.zeros((4, 128, 512), np.float32)
    for r in range(4):
        m = np.full((128, 512), -1e30, np.float32)
        m[:, :r * 128] = 0.0
        blk = np.triu(np.full((128, 128), -1e30, np.float32), 1)
        m[:, r * 128:(r + 1) * 128] = blk
        dmask[r] = m
    triu = np.triu(np.ones((128, 128), np.float32))
    ident = np.eye(128, dtype=np.float32)
    abg = np.stack([np.asarray(ins["L3_attn_alpha"]), np.asarray(ins["L3_attn_beta"]),
                    np.asarray(ins["L3_attn_gamma"])]).reshape(1, 3).astype(np.float32)
    out = []
    for cc in range(NCORES):
        hh = cc % 2
        crole = np.full((128, 1), -1e30 if hh == 0 else 0.0, np.float32)
        flag = np.full((128, 1), float(hh), np.float32)
        tglob = hh * T + np.arange(T, dtype=np.float32)
        rtab = np.ascontiguousarray((1.0 / (tglob + 1.0)).reshape(16, 128).T
                                    .astype(np.float32))
        out.append(dict(xloc=x_fm_cores[cc], xoth=x_fm_cores[cc ^ 1], Wa=Wa,
                        dmask=dmask, crole=crole, flag=flag, rtab=rtab, abg=abg,
                        triu=triu, ident=ident,
                        ident2=np.ascontiguousarray(
                            np.vstack([np.eye(64), np.eye(64)]).astype(np.float32))))
    return out


def build_a1(phase=99):
    nc = bass.Bass("TRN2", target_bir_lowering=False)
    stack = ExitStack()
    c = Ctx(nc, stack)
    p = P()
    c.setup_consts([LN_EPS, 0.0])

    xloc = c.dram_in("xloc", [D, T])
    xoth = c.dram_in("xoth", [D, T])
    Wqkv = c.dram_in("Wqkv", [D, 1536])
    kpos_t = c.dram_in("kpos_t", [4, 128, 8])
    vpos_t = c.dram_in("vpos_t", [4, 128, 8])
    memk = c.dram_in("memk", [4, 64])
    memv = c.dram_in("memv", [4, 64])
    kcw1 = c.dram_in("kcw1", [1024, 1024])
    kcb1_t = c.dram_in("kcb1_t", [128, 8])
    kcw2 = c.dram_in("kcw2", [1024, 64])
    kcb2_c = c.dram_in("kcb2_c", [64, 1])
    vcw1 = c.dram_in("vcw1", [1024, 1024])
    vcb1_t = c.dram_in("vcb1_t", [128, 8])
    vcw2 = c.dram_in("vcw2", [1024, 64])
    vcb2_c = c.dram_in("vcb2_c", [64, 1])
    gatew = c.dram_in("gatew", [D, 32])
    gateb = c.dram_in("gateb", [1, 32])
    outw = c.dram_in("outw", [D, D])
    ctab = c.dram_in("ctab", [128, TF])
    stab = c.dram_in("stab", [128, TF])
    perm_in = c.dram_in("perm", [128, 128])
    ident_in = c.dram_in("ident", [128, 128])
    ident2_in = c.dram_in("ident2", [128, 64])
    xout = c.dram_out("xout", [D, T])
    scrk = nc.dram_tensor("scrk_a1", [TF, 64], BF16, kind="Internal").ap()

    psF = c.ps("psF", [128, TF])
    psA = psF[:, 0:T]
    psB = psF[:, T:2 * T]
    ones_col = c.sb("ones_col", [128, 1], BF16)
    ones_row = c.sb("ones_row", [1, 128], F32)
    identb = c.sb("identb", [128, 128], BF16)
    ident2b = c.sb("ident2b", [128, 64], BF16)
    Qa = c.sb("Qa", [128, CD * T], BF16)
    Ka = c.sb("Ka", [128, 2 * TF], BF16)
    Va = c.sb("Va", [128, 2 * TF], BF16)
    GTS = c.sb("GTS", [128, 16 * 32], F32)
    gb_row = c.sb("gb_row", [1, 32], F32)
    Ws = c.sb("Ws", [128, CD * 128], BF16)

    stack2 = ExitStack()
    c2 = Ctx(nc, stack2)
    Xb = c2.sb("Xb", [128, CD * TF], BF16)
    lnrow0 = c2.sb("lnrow0", [1, T], F32)
    Ct = c2.sb("Ct", [128, TF], BF16)
    St = c2.sb("St", [128, TF], BF16)
    permb = c2.sb("permb", [128, 128], BF16)

    s_init = p.step("vector", lambda v: (v.memset(ones_col[:], 1.0),
                                         v.memset(ones_row[:], 1.0))[-1])
    s_cst = p.step("vector", lambda v: c.const_init_fn(v))
    s_ldc = p.dma("gpsimd", lambda e, ds: (
        e.dma_start(out=identb[:], in_=ident_in).then_inc(ds, 16),
        e.dma_start(out=ident2b[:], in_=ident2_in).then_inc(ds, 16),
        e.dma_start(out=Ct[:], in_=ctab).then_inc(ds, 16),
        e.dma_start(out=St[:], in_=stab).then_inc(ds, 16),
        e.dma_start(out=permb[:], in_=perm_in).then_inc(ds, 16),
        e.dma_start(out=gb_row[:], in_=gateb).then_inc(ds, 16))[-1], 6)

    s_ldl = p.dma("gpsimd", lambda e, ds: e.dma_start(
        out=Xb[:].rearrange("p (c t) -> p c t", t=TF)[:, :, 0:T],
        in_=xloc.rearrange("(c p) t -> p c t", p=128)).then_inc(ds, 16), 1)
    s_ldo = p.dma("gpsimd", lambda e, ds: e.dma_start(
        out=Xb[:].rearrange("p (c t) -> p c t", t=TF)[:, :, T:2 * T],
        in_=xoth.rearrange("(c p) t -> p c t", p=128)).then_inc(ds, 16), 1)

    s_h1, _ = emit_ln(p, c, None, Xb, "lnl", deps=[s_ldl, s_init, s_cst],
                      psA=psA, psB=psB, ones_col=ones_col, ones_row=ones_row,
                      scratch=Qa, xb_stride=TF, xb_off=0, row_buf=lnrow0,
                      skip_cast=True)
    s_h2, _ = emit_ln(p, c, None, Xb, "lno", deps=[s_ldo, s_init, s_cst, s_h1],
                      psA=psA, psB=psB, ones_col=ones_col, ones_row=ones_row,
                      scratch=Qa, xb_stride=TF, xb_off=T, skip_cast=True,
                      row_buf=lnrow0)
    s_h = s_h2

    s_ldg = p.dma("gpsimd", lambda e, ds: e.dma_start(
        out=Ws[:, 0:CD * 32].rearrange("p (c f) -> p c f", f=32),
        in_=gatew.rearrange("(c p) f -> p c f", p=128)).then_inc(ds, 16), 1,
        deps=[s_h])
    g_eps = []
    for n in range(16):
        def mmg(t, n=n):
            for ci in range(CD):
                t.matmul(psF[:, (n % 8) * 512: (n % 8) * 512 + 32],
                         Xb[:, ci * TF + n * 128: ci * TF + n * 128 + 128],
                         Ws[:, ci * 32:(ci + 1) * 32],
                         start=(ci == 0), stop=False)
            return t.matmul(psF[:, (n % 8) * 512: (n % 8) * 512 + 32],
                            ones_row[:], gb_row[:], start=False, stop=True)
        s_mm = p.step("tensor", mmg, deps=[s_ldg, s_ldc, s_h] +
                      ([g_eps[2 * (n - 8) + 1]] if n >= 8 else []))
        s_ep = p.step("scalar", (lambda s, n=n: s.activation(
            out=GTS[:, n * 32:(n + 1) * 32],
            in_=psF[:, (n % 8) * 512: (n % 8) * 512 + 32],
            func=AF.Sigmoid, bias=0.0, scale=1.0)), deps=[s_mm])
        g_eps.extend([s_mm, s_ep])
    g_last = g_eps[-1]

    qkv_eps = [g_last]
    prev_lw = None
    for dst, col0, nch, ntok in ((Ka, 1024, 2, TF), (Va, 1280, 2, TF), (Qa, 0, 8, T)):
        for ch in range(nch):
            def ldw(e, ds, col0=col0, ch=ch):
                return e.dma_start(out=Ws[:].rearrange("p (c f) -> p c f", f=128),
                                   in_=Wqkv[:, col0 + ch * 128: col0 + (ch + 1) * 128]
                                   .rearrange("(c p) f -> p c f", p=128)).then_inc(ds, 16)
            s_lw = p.dma("gpsimd", ldw, 1,
                         deps=[prev_lw] if prev_lw is not None else [s_ldg, g_last])
            for n in range(ntok // 512):
                bk = (n % 4) * 512
                def mm(t, ch=ch, n=n, bk=bk):
                    last = None
                    for ci in range(CD):
                        last = t.matmul(psF[:, bk:bk + 512],
                                        Ws[:, ci * 128:(ci + 1) * 128],
                                        Xb[:, ci * TF + n * 512: ci * TF + n * 512 + 512],
                                        start=(ci == 0), stop=(ci == CD - 1))
                    return last
                s_mm = p.step("tensor", mm, deps=[s_lw] + qkv_eps[-4:])
                s_ep = p.step("vector", (lambda v, ch=ch, n=n, bk=bk, dst=dst, ntok=ntok:
                              v.tensor_copy(dst[:, ch * ntok + n * 512: ch * ntok + n * 512 + 512],
                                            psF[:, bk:bk + 512])), deps=[s_mm])
                qkv_eps.append(s_ep)
                prev_mm_q = s_mm
            prev_lw = s_mm

    rope_tail = qkv_eps[-1:]
    prev_r = qkv_eps[-1]
    for dst, nch, ntok in ((Qa, 8, T), (Ka, 2, TF)):
        for ch in range(nch):
            def mmsw(t, dst=dst, ch=ch, ntok=ntok):
                last = None
                for n in range(ntok // 512):
                    last = t.matmul(psF[:, (n % 8) * 512:(n % 8) * 512 + 512], permb[:],
                                    dst[:, ch * ntok + n * 512: ch * ntok + n * 512 + 512],
                                    start=True, stop=True)
                return last
            s_sw = p.step("tensor", mmsw, deps=[prev_r, s_ldc] + qkv_eps[-2:])
            s_r1 = p.step("vector", (lambda v, dst=dst, ch=ch, ntok=ntok: v.tensor_tensor(
                dst[:, ch * ntok:(ch + 1) * ntok], dst[:, ch * ntok:(ch + 1) * ntok],
                Ct[:, 0:ntok], ALU.mult)), deps=[s_sw])
            s_r2 = p.step("vector", (lambda v, ntok=ntok: v.tensor_tensor(
                psF[:, 0:ntok], psF[:, 0:ntok], St[:, 0:ntok], ALU.mult)),
                deps=[s_sw, s_r1])
            s_r3 = p.step("vector", (lambda v, dst=dst, ch=ch, ntok=ntok: v.tensor_tensor(
                dst[:, ch * ntok:(ch + 1) * ntok], dst[:, ch * ntok:(ch + 1) * ntok],
                psF[:, 0:ntok], ALU.add)), deps=[s_r2])
            prev_r = s_r3
            rope_tail = [s_r3]

    stack2.close()
    Xf = c.sb("Xf", [128, CD * T], F32)
    s_ldx = p.dma("sync", lambda e, ds: e.dma_start(
        out=Xf[:].rearrange("p (c t) -> p c t", t=T),
        in_=xloc.rearrange("(c p) t -> p c t", p=128)).then_inc(ds, 16), 1,
        deps=rope_tail)
    AVb = c.sb("AVb", [128, CD * T], BF16)
    KG = c.sb("KG", [128, TF], BF16)
    Vtm2 = c.sb("Vtm2", [128, 32 * 64], BF16)
    kwf = c.sb("kwf", [128, 2 * 1024], BF16)
    Z1 = c.sb("Z1", [128, CD * 256], BF16)
    CK2 = c.sb("CK2", [128, 257], BF16)
    CV2 = c.sb("CV2", [64, 257], BF16)
    cvtm = c.sb("cvtm", [128, 3 * 64], BF16)
    cprobs = c.sb("cprobs", [128, 4 * 260], BF16)
    IMP = c.sb("IMP", [128, 256], F32)
    fmask = c.sb("fmask", [128, 128], BF16)
    m8 = c.sb("m8", [128, 8], F32)
    fprobs = c.sb("fprobs", [128, TF], BF16)
    PT = c.sb("PT", [128, 16 * 128], BF16)
    avtmp = c.sb("avtmp", [64, 128], BF16)
    srow = c.sb("srow", [128, 4], F32)
    W1s = c.sb("W1s", [128, CD * 128], BF16)
    W2s = c.sb("W2s", [128, 2 * CD * 64], BF16)
    b1k = c.sb("b1k", [128, 8], F32)
    b1v = c.sb("b1v", [128, 8], F32)
    b2k = c.sb("b2k", [64, 1], F32)
    b2v = c.sb("b2v", [64, 1], F32)
    kposb = c.sb("kposb", [128, 8], F32)
    vposb = c.sb("vposb", [128, 8], F32)

    s_ldm = p.dma("gpsimd", lambda e, ds: (
        e.dma_start(out=W2s[:, 0:CD * 64].rearrange("p (c f) -> p c f", f=64),
                    in_=kcw2.rearrange("(c p) f -> p c f", p=128)).then_inc(ds, 16),
        e.dma_start(out=W2s[:, CD * 64:2 * CD * 64].rearrange("p (c f) -> p c f", f=64),
                    in_=vcw2.rearrange("(c p) f -> p c f", p=128)).then_inc(ds, 16),
        e.dma_start(out=b1k[:], in_=kcb1_t).then_inc(ds, 16),
        e.dma_start(out=b1v[:], in_=vcb1_t).then_inc(ds, 16),
        e.dma_start(out=b2k[:], in_=kcb2_c).then_inc(ds, 16),
        e.dma_start(out=b2v[:], in_=vcb2_c).then_inc(ds, 16))[-1], 6,
        deps=rope_tail)

    out_mv = []
    prev_mv = None
    prev_grp_tail = rope_tail + [s_ldm]
    for g in range(4):
        gch, gb2 = g // 2, (g % 2) * 64
        def ldmem(e, ds, g=g):
            e.dma_start(out=CK2[0:64, 0:1], in_=memk[g][:, None]).then_inc(ds, 16)
            e.dma_start(out=CV2[:, 0:1], in_=memv[g][:, None]).then_inc(ds, 16)
            e.dma_start(out=kposb[:], in_=kpos_t[g]).then_inc(ds, 16)
            return e.dma_start(out=vposb[:], in_=vpos_t[g]).then_inc(ds, 16)
        s_ldme = p.dma("gpsimd", ldmem, 4, deps=prev_grp_tail)
        def dupk(e, ds, gch=gch, gb2=gb2):
            e.dma_start(out=KG[0:64, :], in_=Ka[gb2:gb2 + 64, gch * TF:(gch + 1) * TF])\
                .then_inc(ds, 16)
            return e.dma_start(out=KG[64:128, :],
                               in_=Ka[gb2:gb2 + 64, gch * TF:(gch + 1) * TF]).then_inc(ds, 16)
        s_dk = p.dma("sync", dupk, 2, deps=prev_grp_tail)

        which_tail = [s_ldme, s_dk]
        comp_last = None
        for which in range(2):
            KV = Ka if which == 0 else Va
            posb = kposb if which == 0 else vposb
            b1x = b1k if which == 0 else b1v
            b2x = b2k if which == 0 else b2v
            CX = CK2 if which == 0 else CV2
            w1src = kcw1 if which == 0 else vcw1
            tm = Vtm2
            prev_ev = None
            tr_eps = []
            for g0 in range(0, 32, 4):
                def trk(t, g0=g0, KV=KV, gch=gch, gb2=gb2):
                    last = None
                    for j in range(4):
                        g2 = g0 + j
                        last = t.transpose(
                            psF.bitcast(BF16)[:, j * 1024: j * 1024 + 64],
                            KV[gb2:gb2 + 64, gch * TF + g2 * 128: gch * TF + g2 * 128 + 128],
                            ident2b[gb2:gb2 + 64, :])
                    return last
                dd = which_tail + ([prev_ev] if prev_ev is not None else []) + \
                     ([comp_last] if comp_last is not None else [])
                s_tr = p.step("tensor", trk, deps=dd)
                def evk(v, g0=g0, tm=tm):
                    last = None
                    for j in range(4):
                        last = v.tensor_copy(tm[:, (g0 + j) * 64:(g0 + j + 1) * 64],
                                             psF.bitcast(BF16)[:, j * 1024: j * 1024 + 64])
                    return last
                s_ev = p.step("vector", evk, deps=[s_tr])
                prev_ev = s_ev
                tr_eps.append(s_ev)
            s_bo = p.dma("sync", (lambda e, ds, tm=tm: e.dma_start(
                out=scrk.rearrange("(g p) f -> p g f", p=128),
                in_=tm[:].rearrange("p (g f) -> p g f", f=64)).then_inc(ds, 16)), 1,
                deps=tr_eps)
            s_bi = p.dma("sync", lambda e, ds: e.dma_start(
                out=kwf[:].rearrange("p (w q) -> p w q", q=1024),
                in_=scrk.rearrange("(wt p x) d -> p wt (x d)", p=128, x=16)
                ).then_inc(ds, 16), 1, deps=[s_bo])
            fm_eps = []
            prev_ev2 = None
            for wt in range(2):
                for c0 in range(0, 8, 4):
                    def trw(t, wt=wt, c0=c0):
                        last = None
                        for j in range(4):
                            ci = c0 + j
                            last = t.transpose(
                                psF.bitcast(BF16)[:, j * 1024 + 256: j * 1024 + 384],
                                kwf[:, wt * 1024 + ci * 128: wt * 1024 + ci * 128 + 128],
                                identb[:])
                        return last
                    dd = [s_bi] + ([prev_ev2] if prev_ev2 is not None else [])
                    s_t2 = p.step("tensor", trw, deps=dd)
                    def evw(v, wt=wt, c0=c0):
                        last = None
                        for j in range(4):
                            ci = c0 + j
                            last = v.tensor_copy(
                                Z1[:, ci * 256 + wt * 128: ci * 256 + wt * 128 + 128],
                                psF.bitcast(BF16)[:, j * 1024 + 256: j * 1024 + 384])
                        return last
                    s_e2 = p.step("vector", evw, deps=[s_t2])
                    prev_ev2 = s_e2
                    fm_eps.append(s_e2)
            def addpos(v, posb=posb):
                last = None
                for ci in range(CD):
                    last = v.tensor_scalar(Z1[:, ci * 256:(ci + 1) * 256],
                                           Z1[:, ci * 256:(ci + 1) * 256],
                                           posb[:, ci:ci + 1], None, ALU.add)
                return last
            s_ap = p.step("vector", addpos, deps=fm_eps + [s_ldme])
            l1_eps = []
            prev_lw1 = None
            for m in range(CD):
                def ldw1(e, ds, m=m, w1src=w1src):
                    return e.dma_start(out=W1s[:].rearrange("p (c f) -> p c f", f=128),
                                       in_=w1src[:, m * 128:(m + 1) * 128]
                                       .rearrange("(c p) f -> p c f", p=128)).then_inc(ds, 16)
                dd = [prev_lw1] if prev_lw1 is not None else [s_ap]
                s_lw1 = p.dma("gpsimd", ldw1, 1, deps=dd)
                def mm1(t, m=m):
                    last = None
                    for ci in range(CD):
                        last = t.matmul(psF[:, (m % 4) * 512: (m % 4) * 512 + 256],
                                        W1s[:, ci * 128:(ci + 1) * 128],
                                        Z1[:, ci * 256:(ci + 1) * 256],
                                        start=(ci == 0), stop=(ci == CD - 1))
                    return last
                s_m1 = p.step("tensor", mm1, deps=[s_lw1, s_ap] +
                              ([l1_eps[m - 4]] if m >= 4 else []))
                prev_lw1 = s_m1
                s_e1 = p.step("scalar", (lambda s, m=m, b1x=b1x: s.activation(
                    out=kwf[:, m * 256:(m + 1) * 256],
                    in_=psF[:, (m % 4) * 512: (m % 4) * 512 + 256],
                    func=AF.Relu, bias=b1x[:, m:m + 1], scale=1.0)), deps=[s_m1])
                l1_eps.append(s_e1)
            def mm2(t, which=which):
                last = None
                for ci in range(CD):
                    last = t.matmul(psF[0:64, 2048:2048 + 256],
                                    W2s[:, which * CD * 64 + ci * 64:
                                        which * CD * 64 + (ci + 1) * 64],
                                    kwf[:, ci * 256:(ci + 1) * 256],
                                    start=(ci == 0), stop=(ci == CD - 1))
                return last
            s_m2 = p.step("tensor", mm2, deps=l1_eps[-4:] + [s_ldm])
            s_e2b = p.step("scalar", (lambda s, b2x=b2x, CX=CX: s.activation(
                out=CX[0:64, 1:257] if CX is CK2 else CX[:, 1:257],
                in_=psF[0:64, 2048:2048 + 256],
                func=AF.Identity, bias=b2x[:], scale=1.0)), deps=[s_m2])
            comp_last = s_e2b
            which_tail = [s_e2b]

        s_dck = p.dma("sync", lambda e, ds: e.dma_start(
            out=CK2[64:128, :], in_=CK2[0:64, :]).then_inc(ds, 16), 1,
            deps=[comp_last, s_ldme])
        def trcv(t):
            t.transpose(psF.bitcast(BF16)[:, 256:320], CV2[:, 0:128], ident2b[0:64, :])
            t.transpose(psF.bitcast(BF16)[:, 1024 + 256:1024 + 320], CV2[:, 128:256],
                        ident2b[0:64, :])
            return t.transpose(psF.bitcast(BF16)[0:1, 2048 + 256:2048 + 320],
                               CV2[:, 256:257], ident2b[0:64, :])
        s_tcv = p.step("tensor", trcv, deps=[comp_last])
        def evcv(v):
            v.tensor_copy(cvtm[:, 0:64], psF.bitcast(BF16)[:, 256:320])
            v.tensor_copy(cvtm[:, 64:128], psF.bitcast(BF16)[:, 1024 + 256:1024 + 320])
            return v.tensor_copy(cvtm[0:1, 128:192],
                                 psF.bitcast(BF16)[0:1, 2048 + 256:2048 + 320])
        s_ecv = p.step("vector", evcv, deps=[s_tcv])

        grp_av_tail = []
        prev_qt = None
        prev_mv = None if g == 0 else prev_mv
        for qt in range(16):
            cs_last = None
            s_imp = None
            cn_steps = []
            for j in range(4):
                hq = 4 * g + j
                qch, qb2 = hq // 2, (hq % 2) * 64
                def mmcs(t, qch=qch, qb2=qb2, qt=qt, j=j):
                    return t.matmul(psF[:, j * 512: j * 512 + 257],
                                    Qa[qb2:qb2 + 64, qch * T + qt * 128: qch * T + qt * 128 + 128],
                                    CK2[qb2:qb2 + 64, 0:257], start=True, stop=True)
                dd = [s_ecv, s_dck] + ([prev_qt] if prev_qt is not None else [])
                if cs_last is not None:
                    dd.append(cs_last)
                s_cs = p.step("tensor", mmcs, deps=dd)
                cs_last = s_cs
                if j == 0:
                    s_imp = p.step("vector", (lambda v, j=j: v.tensor_copy(
                        IMP[:], psF[:, j * 512 + 1: j * 512 + 257])), deps=[s_cs])
                else:
                    s_imp = p.step("vector", (lambda v, j=j: v.tensor_tensor(
                        IMP[:], IMP[:], psF[:, j * 512 + 1: j * 512 + 257], ALU.add)),
                        deps=[s_cs, s_imp])
                s_cmx = p.step("vector", (lambda v, j=j: v.tensor_reduce(
                    srow[:, 0:1], psF[:, j * 512: j * 512 + 257], AX.X, ALU.max,
                    negate=True)), deps=[s_imp])
                s_cms = p.step("gpsimd", (lambda gg: gg.tensor_scalar_mul(
                    srow[:, 0:1], srow[:, 0:1], 0.125)), deps=[s_cmx])
                s_cex = p.step("scalar", (lambda s, j=j: s.activation(
                    out=cprobs[:, j * 260: j * 260 + 257],
                    in_=psF[:, j * 512: j * 512 + 257], func=AF.Exp,
                    bias=srow[:, 0:1], scale=0.125, accum_out=srow[:, 1:2])),
                    deps=[s_cms])
                s_crc = p.step("vector", lambda v: v.reciprocal(srow[:, 1:2], srow[:, 1:2]),
                               deps=[s_cex])
                s_cnm = p.step("vector", (lambda v, j=j, hq=hq, qt=qt: v.tensor_scalar(
                    cprobs[:, j * 260: j * 260 + 257], cprobs[:, j * 260: j * 260 + 257],
                    srow[:, 1:2], GTS[:, qt * 32 + 2 * hq: qt * 32 + 2 * hq + 1],
                    ALU.mult, ALU.mult)), deps=[s_crc])
                cn_steps.append(s_cnm)
            s_i2 = p.step("vector", lambda v: v.tensor_tensor(
                IMP[:, 0:256:2], IMP[:, 0:256:2], IMP[:, 1:256:2], ALU.add),
                deps=[s_imp])
            s_m8 = p.step("vector", lambda v: v.max(m8[:], IMP[:, 0:256:2]),
                          deps=[s_i2])
            s_fm = p.step("vector", lambda v: v.tensor_scalar(
                fmask[:], IMP[:, 0:256:2], m8[:, 7:8], None, ALU.is_ge), deps=[s_m8])
            prev_j_tail = cn_steps[-1:]
            for j in range(4):
                hq = 4 * g + j
                qch, qb2 = hq // 2, (hq % 2) * 64
                def mmfs(t, qch=qch, qb2=qb2, qt=qt):
                    last = None
                    for kc in range(8):
                        last = t.matmul(psF[:, kc * 512:(kc + 1) * 512],
                                        Qa[qb2:qb2 + 64, qch * T + qt * 128: qch * T + qt * 128 + 128],
                                        KG[qb2:qb2 + 64, kc * 512:(kc + 1) * 512],
                                        start=True, stop=True)
                    return last
                s_fs = p.step("tensor", mmfs, deps=[s_fm] + prev_j_tail + [s_dk])
                s_fmx = p.step("vector", (lambda v: v.tensor_reduce(
                    srow[:, 2:3], psF[:, 0:TF], AX.X, ALU.max, negate=True)),
                    deps=[s_fs])
                s_fms = p.step("gpsimd", (lambda gg: gg.tensor_scalar_mul(
                    srow[:, 2:3], srow[:, 2:3], 0.125)), deps=[s_fmx])
                s_fex = p.step("scalar", (lambda s: s.activation(
                    out=fprobs[:], in_=psF[:, 0:TF], func=AF.Exp, bias=srow[:, 2:3],
                    scale=0.125)), deps=[s_fms])
                s_fmk = p.step("vector", (lambda v: v.tensor_tensor(
                    fprobs[:].rearrange("p (b w) -> p b w", w=32),
                    fprobs[:].rearrange("p (b w) -> p b w", w=32),
                    fmask[:, :, None].broadcast_to([128, 128, 32]), ALU.mult)),
                    deps=[s_fex, s_fm])
                s_fsm = p.step("vector", (lambda v: v.tensor_reduce(
                    srow[:, 3:4], fprobs[:], AX.X, ALU.add)), deps=[s_fmk])
                s_frc = p.step("vector", lambda v: v.reciprocal(srow[:, 3:4], srow[:, 3:4]),
                               deps=[s_fsm])
                s_fnm = p.step("vector", (lambda v, hq=hq, qt=qt: v.tensor_scalar(
                    fprobs[:], fprobs[:], srow[:, 3:4],
                    GTS[:, qt * 32 + 2 * hq + 1: qt * 32 + 2 * hq + 2],
                    ALU.mult, ALU.mult)), deps=[s_frc])
                def trc(t, j=j):
                    t.transpose(psF.bitcast(BF16)[:, 256:384],
                                cprobs[:, j * 260: j * 260 + 128], identb[:])
                    t.transpose(psF.bitcast(BF16)[:, 1024 + 256:1024 + 384],
                                cprobs[:, j * 260 + 128: j * 260 + 256], identb[:])
                    return t.transpose(psF.bitcast(BF16)[0:1, 2048 + 256: 2048 + 384],
                                       cprobs[:, j * 260 + 256: j * 260 + 257], identb[:])
                s_tc = p.step("tensor", trc, deps=[s_fnm])
                def evc(v):
                    v.tensor_copy(PT[:, 0:128], psF.bitcast(BF16)[:, 256:384])
                    v.tensor_copy(PT[:, 128:256], psF.bitcast(BF16)[:, 1024 + 256:1024 + 384])
                    return v.tensor_copy(PT[0:1, 256:384],
                                         psF.bitcast(BF16)[0:1, 2048 + 256:2048 + 384])
                s_ec = p.step("vector", evc, deps=[s_tc])
                def mmcav(t):
                    t.matmul(psF[0:64, 0:128], cvtm[:, 0:64], PT[:, 0:128],
                             start=True, stop=False)
                    t.matmul(psF[0:64, 0:128], cvtm[:, 64:128], PT[:, 128:256],
                             start=False, stop=False)
                    return t.matmul(psF[0:64, 0:128], cvtm[0:1, 128:192],
                                    PT[0:1, 256:384], start=False, stop=True)
                s_cav = p.step("tensor", mmcav, deps=[s_ec])
                s_ae0 = p.step("vector", (lambda v: v.tensor_copy(
                    avtmp[:], psF[0:64, 0:128])),
                    deps=[s_cav] + ([prev_mv] if prev_mv is not None else []))
                s_av = s_cav
                prev_ev3 = s_ec
                for wv in range(2):
                    ev_steps = []
                    for gr in range(2):
                        def trgf(t, wv=wv, gr=gr):
                            last = None
                            for jj in range(8):
                                last = t.transpose(
                                    psF.bitcast(BF16)[:, jj * 1024 + 384: jj * 1024 + 512],
                                    fprobs[:, (wv * 16 + gr * 8 + jj) * 128:
                                           (wv * 16 + gr * 8 + jj + 1) * 128],
                                    identb[:])
                            return last
                        dd2 = [s_fnm, prev_ev3, s_av] + \
                              ([s_ae0] if wv == 0 else [s_aeW])
                        s_trf = p.step("tensor", trgf, deps=dd2)
                        def evgf(v, gr=gr):
                            last = None
                            for jj in range(8):
                                last = v.tensor_copy(
                                    PT[:, (gr * 8 + jj) * 128: (gr * 8 + jj + 1) * 128],
                                    psF.bitcast(BF16)[:, jj * 1024 + 384: jj * 1024 + 512])
                            return last
                        s_evf = p.step("vector", evgf, deps=[s_trf])
                        prev_ev3 = s_evf
                        ev_steps.append(s_evf)
                    def mmfav(t, wv=wv):
                        last = None
                        for g2 in range(16):
                            last = t.matmul(
                                psF[0:64, 0:128],
                                Vtm2[:, (wv * 16 + g2) * 64:(wv * 16 + g2 + 1) * 64],
                                PT[:, g2 * 128:(g2 + 1) * 128],
                                start=(g2 == 0), stop=(g2 == 15))
                        return last
                    s_av = p.step("tensor", mmfav, deps=ev_steps + [s_ae0 if wv == 0 else s_aeW])
                    s_aeW = p.step("vector", (lambda v: v.tensor_tensor(
                        avtmp[:], avtmp[:], psF[0:64, 0:128], ALU.add)), deps=[s_av])
                s_mv = p.dma("sync", (lambda e, ds, hq=hq, qt=qt: e.dma_start(
                    out=AVb[(hq % 2) * 64:(hq % 2) * 64 + 64,
                            (hq // 2) * T + qt * 128: (hq // 2) * T + qt * 128 + 128],
                    in_=avtmp[:]).then_inc(ds, 16)), 1, deps=[s_aeW])
                prev_mv = s_mv
                prev_j_tail = [s_aeW]
                prev_qt = s_aeW
            grp_av_tail.append(prev_qt)
        out_mv.append(prev_mv)
        prev_grp_tail = [prev_mv]

    if phase == 5:
        s_cpd = p.step("vector", lambda v: v.tensor_copy(Xf[:], AVb[:]), deps=out_mv + [s_ldx])
        s_st = p.dma("sync", lambda e, ds: e.dma_start(
            out=xout.rearrange("(c p) t -> p c t", p=128),
            in_=Xf[:].rearrange("p (c t) -> p c t", t=T)).then_inc(ds, 16), 1,
            deps=[s_cpd])
        p.step("sync", lambda e: e.nop(), deps=[s_st])
        p.lower(nc, stack)
        return nc, stack
    prev_lw = None
    prev_mm_o = None
    oeps = []
    for m in range(CD):
        def ldwp(e, ds, m=m):
            return e.dma_start(out=Ws[:].rearrange("p (c f) -> p c f", f=128),
                               in_=outw[:, m * 128:(m + 1) * 128]
                               .rearrange("(c p) f -> p c f", p=128)).then_inc(ds, 16)
        s_lw = p.dma("gpsimd", ldwp, 1,
                     deps=([prev_mm_o] if prev_lw is not None else out_mv))
        prev_lw = s_lw
        for n in range(NT):
            def mmo(t, m=m, n=n):
                last = None
                for ci in range(CD):
                    last = t.matmul(
                        psF[:, ((m * NT + n) % 4) * 512: ((m * NT + n) % 4) * 512 + 512],
                        Ws[:, ci * 128:(ci + 1) * 128],
                        AVb[:, ci * T + n * 512: ci * T + n * 512 + 512],
                        start=(ci == 0), stop=(ci == CD - 1))
                return last
            gi = m * NT + n
            s_mm = p.step("tensor", mmo,
                          deps=[s_lw] + out_mv[-1:] + ([oeps[gi - 4]] if gi >= 4 else []))
            prev_mm_o = s_mm
            def epo(v, m=m, n=n):
                sl = slice(m * T + n * 512, m * T + n * 512 + 512)
                return v.tensor_tensor(
                    Xf[:, sl],
                    psF[:, ((m * NT + n) % 4) * 512: ((m * NT + n) % 4) * 512 + 512],
                    Xf[:, sl], ALU.add)
            s_ep = p.step("vector", epo, deps=[s_mm, s_ldx])
            oeps.append(s_ep)

    s_st = p.dma("sync", lambda e, ds: e.dma_start(
        out=xout.rearrange("(c p) t -> p c t", p=128),
        in_=Xf[:].rearrange("p (c t) -> p c t", t=T)).then_inc(ds, 16), 1, deps=oeps)
    p.step("sync", lambda e: e.nop(), deps=[s_st])
    p.lower(nc, stack)
    return nc, stack


def _rope_fm_L1_host(seq_len=TF, dim=64):
    inv = 1.0 / 10000.0 ** (np.arange(0, dim, 2, dtype=np.float32) / dim)
    ang = np.arange(seq_len, dtype=np.float32)[:, None] * inv[None, :]
    cos = np.cos(ang)
    sin = np.sin(ang)
    C = np.empty((dim, seq_len), np.float32)
    S = np.empty((dim, seq_len), np.float32)
    C[0::2] = cos.T
    C[1::2] = cos.T
    S[0::2] = -sin.T
    S[1::2] = sin.T
    return C, S


def a1_inputs(x_fm_cores, ins):
    Wqkv = np.ascontiguousarray(ins["L1_attn_w_qkv"], np.float32)
    kpos_t = np.ascontiguousarray(np.asarray(ins["L1_attn_k_pos"], np.float32)
                                  .reshape(4, 1024).reshape(4, 8, 128)
                                  .transpose(0, 2, 1))
    vpos_t = np.ascontiguousarray(np.asarray(ins["L1_attn_v_pos"], np.float32)
                                  .reshape(4, 1024).reshape(4, 8, 128)
                                  .transpose(0, 2, 1))
    mem = np.asarray(ins["L1_attn_mem_kv"], np.float32).reshape(2, 4, 64)
    C, S = _rope_fm_L1_host()
    C2 = np.vstack([C, C])
    S2 = np.vstack([S, S])
    perm = np.zeros((128, 128), np.float32)
    for i in range(64):
        perm[2 * i, 2 * i + 1] = 1.0
        perm[2 * i + 1, 2 * i] = 1.0
    ident = np.eye(128, dtype=np.float32)
    out = []
    for cc in range(NCORES):
        hh = cc % 2
        idx0 = slice(hh * T, (hh + 1) * T)
        idx1 = slice((1 - hh) * T, (2 - hh) * T)
        Cc = np.ascontiguousarray(np.concatenate([C2[:, idx0], C2[:, idx1]], axis=1))
        Sc = np.ascontiguousarray(np.concatenate([S2[:, idx0], S2[:, idx1]], axis=1))
        out.append(dict(
            xloc=x_fm_cores[cc], xoth=x_fm_cores[cc ^ 1], Wqkv=Wqkv,
            kpos_t=kpos_t, vpos_t=vpos_t, memk=np.ascontiguousarray(mem[0]),
            memv=np.ascontiguousarray(mem[1]),
            kcw1=np.ascontiguousarray(ins["L1_attn_kc_w1"], np.float32),
            kcb1_t=_bias_tile(ins["L1_attn_kc_b1"]),
            kcw2=np.ascontiguousarray(ins["L1_attn_kc_w2"], np.float32),
            kcb2_c=np.ascontiguousarray(np.asarray(ins["L1_attn_kc_b2"], np.float32)
                                        .reshape(64, 1)),
            vcw1=np.ascontiguousarray(ins["L1_attn_vc_w1"], np.float32),
            vcb1_t=_bias_tile(ins["L1_attn_vc_b1"]),
            vcw2=np.ascontiguousarray(ins["L1_attn_vc_w2"], np.float32),
            vcb2_c=np.ascontiguousarray(np.asarray(ins["L1_attn_vc_b2"], np.float32)
                                        .reshape(64, 1)),
            gatew=np.ascontiguousarray(ins["L1_attn_gate_w"], np.float32),
            gateb=np.ascontiguousarray(np.asarray(ins["L1_attn_gate_b"], np.float32)
                                       .reshape(1, 32)),
            outw=np.ascontiguousarray(ins["L1_attn_out_w"], np.float32),
            ctab=Cc, stab=Sc, perm=perm, ident=ident,
            ident2=np.ascontiguousarray(np.vstack([np.eye(64), np.eye(64)])
                                        .astype(np.float32))))
    return out


def f0_inputs(x_fm_cores, ins):
    Wr = np.ascontiguousarray(ins["L0_ffn_Wr"], np.float32)
    br = np.ascontiguousarray(ins["L0_ffn_br"], np.float32)
    W1 = np.ascontiguousarray(ins["L0_ffn_W1"], np.float32)
    b1f = np.ascontiguousarray(np.asarray(ins["L0_ffn_b1"]).reshape(8, 16, 128)
                               .transpose(0, 2, 1), np.float32)
    W2 = np.ascontiguousarray(ins["L0_ffn_W2"], np.float32)
    b2f = np.ascontiguousarray(np.asarray(ins["L0_ffn_b2"]).reshape(8, 8, 128)
                               .transpose(0, 2, 1), np.float32)
    ident = np.eye(128, dtype=np.float32)
    return [dict(xin=x_fm_cores[i], Wr=Wr, br=br, W1=W1, b1f=b1f, W2=W2, b2f=b2f,
                 ident=ident) for i in range(NCORES)]


def shard_x(x):
    out = []
    for cc in range(NCORES):
        b, h = cc // 2, cc % 2
        out.append(np.ascontiguousarray(x[b, h * T:(h + 1) * T, :].T))
    return out


def unshard_x(shards):
    x = np.empty((4, TF, D), np.float32)
    for cc in range(NCORES):
        b, h = cc // 2, cc % 2
        x[b, h * T:(h + 1) * T, :] = shards[cc].T
    return x


def kernel(**ins):
    x = np.ascontiguousarray(np.asarray(ins["L0_attn_x"]), np.float32)
    xs = shard_x(x)

    plan = [
        ("f0", build_f0, f0_inputs), ("a0", build_a0, a0_inputs),
        ("f0", build_f0, f0_inputs),
        ("f1", build_f1, f1_inputs), ("a1", build_a1, a1_inputs),
        ("f1", build_f1, f1_inputs),
        ("f2", build_f2, f2_inputs), ("a2", build_a2, a2_inputs),
        ("f2", build_f2, f2_inputs),
        ("f3", build_f3, f3_inputs), ("a3", build_a3, a3_inputs),
        ("f3", build_f3, f3_inputs),
    ]
    built = {}
    try:
        for name, build, inp in plan:
            if name not in built:
                built[name] = build()
            nc, _stack = built[name]
            res = run_bass_kernel_spmd(nc, inp(xs, ins), core_ids=list(range(NCORES)))
            xs = [np.ascontiguousarray(np.asarray(r["xout"])) for r in res.results]
    finally:
        for nc, stack in built.values():
            stack.close()
    return unshard_x(xs)
